# version 01 (final)
# baseline (speedup 1.0000x reference)
import contextlib
import numpy as np
import ml_dtypes
import concourse.bass as bass
import concourse.mybir as mybir
from concourse.bass_utils import run_bass_kernel_spmd

F32 = mybir.dt.float32
BF16 = mybir.dt.bfloat16
AF = mybir.ActivationFunctionType
ALU = mybir.AluOpType
AX = mybir.AxisListType
BF = ml_dtypes.bfloat16

NCORE, SEQ, D = 8, 4096, 1024
LN_EPS = 1e-5
M_INTER, M_NST, M_HEADS, M_P = 2048, 128, 32, 64
M_CONVD = M_INTER + 2 * M_NST
M_PROJ = M_INTER + M_CONVD + M_HEADS
TILE = 256
NT = SEQ // TILE

ENGS = ("te", "ve", "se", "sy", "gp")
DUMPS = {}
DUMP_TILE = 0


def add_dump(nc, S, name, ap, rkeys):
    return None


class Op:
    __slots__ = ("eng", "fn", "deps", "label", "idx", "dma", "inc", "cum", "sem")


class Sched:
    def __init__(self):
        self.ops = []
        self.writer = {}
        self.readers = {}
        self.bylabel = {}
        self.paranoid = False

    def fence(self):
        last = {}
        for op in self.ops:
            last[(op.eng, op.dma)] = op.idx
        fid = []
        for e in ENGS:
            deps = [i for k, i in last.items()]
            def fn(eng):
                return eng.nop()
            fid.append(self.add(e, fn, after=(), label=None, _deps=deps))
        return fid

    def add(self, eng, fn, r=(), w=(), sr=(), label=None, dma=False, after=(), _deps=()):
        deps = []
        for key in r:
            if key in self.writer:
                deps.append((self.writer[key], self.paranoid))
        for key in sr:
            if key in self.writer:
                deps.append((self.writer[key], True))
        for key in w:
            if key in self.writer:
                deps.append((self.writer[key], self.paranoid))
            for ri in self.readers.get(key, ()):
                deps.append((ri, self.paranoid))
        for lab in after:
            deps.append((self.bylabel[lab], False))
        for di in _deps:
            deps.append((di, False))
        idx = len(self.ops)
        for key in r:
            self.readers.setdefault(key, []).append(idx)
        for key in sr:
            self.readers.setdefault(key, []).append(idx)
        for key in w:
            self.writer[key] = idx
            self.readers[key] = []
        op = Op()
        op.eng, op.fn, op.label, op.dma = eng, fn, label, dma
        op.deps = sorted(set(deps))
        op.inc = False
        op.idx = idx
        self.ops.append(op)
        if label is not None:
            assert label not in self.bylabel, label
            self.bylabel[label] = idx
        return idx

    NDQ = 4

    def emit(self, nc, block, sems):
        for op in self.ops:
            if op.dma:
                op.inc = True
            for di, forced in op.deps:
                dep = self.ops[di]
                if forced or dep.eng != op.eng:
                    dep.inc = True
        cnt = {e: 0 for e in ENGS}
        dcnt = {}
        for op in self.ops:
            if op.dma:
                assert getattr(op.fn, "dma_n", 1) == 1, "one dma per op"
                k = dcnt.get(op.eng, 0)
                dcnt[op.eng] = k + 1
                sname = f"d_{op.eng}_{k % Sched.NDQ}"
                if sname not in sems:
                    sems[sname] = nc.alloc_semaphore(sname)
                op.sem = sname
                op.cum = (k // Sched.NDQ + 1) * 16
            else:
                if op.inc:
                    cnt[op.eng] += 1
                op.sem = op.eng
                op.cum = cnt[op.eng]

        def section(eng_name):
            def run(eng):
                hi = {}
                for op in self.ops:
                    if op.eng != eng_name:
                        continue
                    for di, forced in op.deps:
                        dep = self.ops[di]
                        if dep.eng == eng_name and not forced and not dep.dma:
                            continue
                        assert dep.inc, (op.label or op.idx, dep.label or dep.idx)
                        if dep.cum > hi.get(dep.sem, 0):
                            eng.wait_ge(sems[dep.sem], dep.cum)
                            hi[dep.sem] = dep.cum
                    if op.dma and op.cum > 16:
                        if op.cum - 16 > hi.get(op.sem, 0):
                            eng.wait_ge(sems[op.sem], op.cum - 16)
                            hi[op.sem] = op.cum - 16
                    ret = op.fn(eng)
                    if op.inc:
                        if op.dma:
                            ins = ret[-1] if isinstance(ret, (list, tuple)) else ret
                            ins.then_inc(sems[op.sem], 16)
                        else:
                            ins = ret[-1] if isinstance(ret, (list, tuple)) else ret
                            assert ins is not None, op.label
                            ins.then_inc(sems[eng_name], 1)
            return run

        block.tensor(section("te"))
        block.vector(section("ve"))
        block.scalar(section("se"))
        block.sync(section("sy"))
        block.gpsimd(section("gp"))


class WStage:

    SW = 512

    def __init__(self, nc, name, ap):
        R, C = ap.shape
        self.name, self.R, self.C, self.ap = name, R, C, ap
        assert C % 128 == 0
        self.nch = C // 128
        self.nsl = (R + self.SW - 1) // self.SW
        self.scr = nc.dram_tensor(f"wscr_{name}", [self.nsl, self.nch * self.SW, 128],
                                  BF16, kind="Internal").ap()

    def emit_casts(self, S):
        for sl in range(self.nsl):
            w = min(self.SW, self.R - sl * self.SW)
            ngrp = (self.nch + 15) // 16
            for gi in range(ngrp):
                c0, c1 = gi * 16, min((gi + 1) * 16, self.nch)
                src = self.ap[sl * self.SW: sl * self.SW + w,
                              c0 * 128:c1 * 128].rearrange(
                    "r (ch k) -> ch r k", k=128)
                dst = self.scr[sl].rearrange("(ch r) k -> ch r k",
                                             r=self.SW)[c0:c1, 0:w, :]

                def fn(g, src=src, dst=dst):
                    return g.dma_start(out=dst, in_=src)
                fn.dma_n = 1
                S.add("gp", fn, w=(f"wscr_{self.name}_{sl}",),
                      label=f"wcast_{self.name}_{sl}_g{gi}", dma=True)

    def load_slice(self, S, eng, dest_ap, sl, bufkey, label=None):
        def fn(e, dest_ap=dest_ap, sl=sl):
            return e.dma_start_transpose(dest_ap, self.scr[sl])
        fn.dma_n = 1
        S.add(eng, fn, r=(f"wscr_{self.name}_{sl}",), w=(bufkey,), label=label,
              dma=True)


def make_consts():
    c = {}
    c["identity_bf"] = np.eye(128, dtype=BF)
    c["identity_f32"] = np.eye(128, dtype=np.float32)
    c["ones_f32"] = np.ones((1, 128), dtype=np.float32)
    c["onescol_bf"] = np.ones((128, 1), dtype=BF)
    c["ones_bf"] = np.ones((1, 128), dtype=BF)
    c["onescol_f32"] = np.ones((128, 1), dtype=np.float32)
    c["triu_f32"] = np.triu(np.ones((128, 128), dtype=np.float32))
    c["negones_f32"] = -np.ones((1, 512), dtype=np.float32)
    ti = np.arange(256)[None, :]
    tj = np.arange(128)[:, None]
    c["eps_ln"] = np.full((128, 1), 1e-5, dtype=np.float32)
    invf = (1.0 / 10000.0 ** (np.arange(0, 64, 2, dtype=np.float32) / 64.0))
    dpat = np.arange(128) % 64
    fr = invf[dpat % 32]
    tt_ = np.arange(SEQ, dtype=np.float32)
    ang = np.float32(fr[:, None]) * tt_[None, :]
    c["rope_cos"] = np.cos(ang).astype(BF)
    c["rope_sin"] = np.sin(ang).astype(BF)
    R = np.zeros((128, 128), dtype=np.float32)
    for blk in range(2):
        for dd in range(32):
            R[blk * 64 + dd, blk * 64 + 32 + dd] = -1.0
            R[blk * 64 + 32 + dd, blk * 64 + dd] = 1.0
    c["rotPT"] = R.T.astype(BF)
    qi = np.arange(512)[None, :]
    m = np.zeros((128, 8, 512), dtype=np.float32)
    for kt in range(8):
        kp = np.arange(128)[:, None]
        if kt < 4:
            kl = kt * 128 + kp
            m[:, kt] = (qi <= kl)
        else:
            ki = (kt - 4) * 128 + kp
            m[:, kt] = (qi >= ki)
    c["lmask"] = m.reshape(128, 8 * 512).astype(BF)
    lk = np.concatenate([np.full(14, np.log(1152.0)), np.full(14, np.log(288.0))])
    c["sel_logk"] = lk[None, :].astype(np.float32)
    c["sel_kth"] = np.concatenate([np.full(14, 1024.0), np.full(14, 256.0)])[None, :].astype(np.float32)
    c["iota_tm"] = (np.arange(28)[None, :] * 128 + np.arange(128)[:, None]).astype(np.float32)
    c["ish23"] = np.full((1, 28), 23, dtype=np.int32)
    c["expmask"] = np.full((1, 28), np.int32(-8388608), dtype=np.int32)
    c["nl2b"] = np.full((1, 1), -126.5 * np.log(2.0), dtype=np.float32)
    c["ssd_m0"] = (ti >= tj).astype(BF)
    c["ssd_m1"] = (ti >= tj + 128).astype(BF)
    return c


def prep_params(inputs):
    p = {}
    cw = np.asarray(inputs["L0_attn_conv_w"])[:, 0, :]
    p["m_convw"] = np.ascontiguousarray(
        cw.reshape(18, 128, 4).transpose(1, 0, 2).reshape(128, 72)).astype(np.float32)
    p["m_convb"] = np.ascontiguousarray(
        np.asarray(inputs["L0_attn_conv_b"]).reshape(18, 128).T).astype(np.float32)
    p["m_dtbias"] = np.asarray(inputs["L0_attn_dt_bias"]).reshape(32, 1).astype(np.float32)
    p["m_alog"] = np.asarray(inputs["L0_attn_A_log"]).reshape(32, 1).astype(np.float32)
    p["m_D"] = np.asarray(inputs["L0_attn_D"]).reshape(1, 32).astype(np.float32)
    p["m_lnw"] = np.ascontiguousarray(
        np.asarray(inputs["L0_attn_ln_w"]).reshape(16, 128).T).astype(np.float32)
    p["m_lnb"] = np.ascontiguousarray(
        np.asarray(inputs["L0_attn_ln_b"]).reshape(16, 128).T).astype(np.float32)
    p["f_lng"] = np.tile(np.asarray(inputs["L0_ffn_ln_g"])[None, :], (128, 1)).astype(BF)
    p["f_lnb"] = np.tile(np.asarray(inputs["L0_ffn_ln_b"])[None, :], (128, 1)).astype(BF)
    rtkv = np.asarray(inputs["L0_ffn_rt_kv"]); rtq = np.asarray(inputs["L0_ffn_rt_q"])
    rt = np.stack([rtkv[0], rtkv[1], rtq[0], rtq[1]], axis=1)
    p["f_rtT"] = np.ascontiguousarray(
        rt.reshape(8, 128, 4).transpose(1, 0, 2).reshape(128, 32)).astype(np.float32)
    qg = np.asarray(inputs["L0_ffn_qnorm_g"], np.float32)
    cg = np.asarray(inputs["L0_ffn_cnorm_g"], np.float32)
    p["r_qg"] = np.ascontiguousarray(
        qg.reshape(2, 8, 128).transpose(2, 0, 1).reshape(128, 16))
    p["r_cg"] = np.ascontiguousarray(
        cg.reshape(2, 8, 128).transpose(2, 0, 1).reshape(128, 16))
    nkv = np.asarray(inputs["L0_ffn_null_kv"], np.float32)
    nk = np.zeros((128, 32), np.float32)
    for g in range(2):
        for h in range(16):
            nk[(h % 2) * 64:(h % 2) * 64 + 64, g * 16 + h] = nkv[0, g, h]
    p["r_nk"] = nk
    p["r_nv"] = np.ascontiguousarray(nkv[1].reshape(1, 2048))
    bq = np.asarray(inputs["L1_attn_b_qkv"], np.float32)
    perm = np.concatenate([np.arange(16)[:, None] * 192 + a * 64 +
                           np.arange(64)[None, :] for a in range(3)], axis=0)
    perm = perm.reshape(3, 16, 64).reshape(-1)
    p["l1_bqkv"] = np.ascontiguousarray(bq[perm][None, :])
    rw = np.asarray(inputs["L1_ffn_router_w"], np.float32)
    p["l1_rwT"] = np.ascontiguousarray(
        rw.T.reshape(8, 128, 16).transpose(1, 0, 2).reshape(128, 128))
    p["l1_rb"] = np.asarray(inputs["L1_ffn_router_b"], np.float32).reshape(1, 16)
    p["l1_b1"] = np.ascontiguousarray(
        np.asarray(inputs["L1_ffn_b1"], np.float32).reshape(32, 128).T)
    p["l1_b2"] = np.asarray(inputs["L1_ffn_b2"], np.float32).reshape(1, 1024)
    return p


PARAM_SHAPES = {
    "m_convw": [128, 72], "m_convb": [128, 18], "m_dtbias": [32, 1],
    "m_alog": [32, 1], "m_D": [1, 32], "m_lnw": [128, 16], "m_lnb": [128, 16],
}
RA_PARAM_SHAPES = {
    "r_qg": [128, 16], "r_cg": [128, 16], "r_nk": [128, 32], "r_nv": [1, 2048],
    "l1_bqkv": [1, 3072], "l1_rwT": [128, 128], "l1_rb": [1, 16],
    "l1_b1": [128, 32], "l1_b2": [1, 1024],
}
PARAM_SHAPES_BF = {
    "f_lng": [128, 1024], "f_lnb": [128, 1024],
}
PARAM_SHAPES_BF2 = {"f_rtT": [128, 32]}


def build(stage=1):
    nc = bass.Bass("TRN2", target_bir_lowering=False)
    x0 = nc.dram_tensor("x0", [SEQ, D], F32, kind="ExternalInput").ap()
    out = nc.dram_tensor("out", [NCORE, 512, D], F32, kind="ExternalOutput").ap()
    consts_np = make_consts()
    def _dt(v):
        if v.dtype == np.float32:
            return F32
        if v.dtype == np.int32:
            return mybir.dt.int32
        return BF16
    cin = {k: nc.dram_tensor("c_" + k, list(v.shape), _dt(v),
                             kind="ExternalInput").ap() for k, v in consts_np.items()}
    pin = {k: nc.dram_tensor(k, shp, F32, kind="ExternalInput").ap()
           for k, shp in PARAM_SHAPES.items()}
    pin.update({k: nc.dram_tensor(k, shp, F32, kind="ExternalInput").ap()
                for k, shp in RA_PARAM_SHAPES.items()})
    pin.update({k: nc.dram_tensor(k, shp, BF16, kind="ExternalInput").ap()
                for k, shp in PARAM_SHAPES_BF.items()})
    pin.update({k: nc.dram_tensor(k, shp, F32, kind="ExternalInput").ap()
                for k, shp in PARAM_SHAPES_BF2.items()})
    w_inproj = nc.dram_tensor("w_inproj", [M_PROJ, D], F32, kind="ExternalInput").ap()
    w_outproj = nc.dram_tensor("w_outproj", [D, M_INTER], F32, kind="ExternalInput").ap()
    ws_in = WStage(nc, "inproj", w_inproj)
    ws_out = WStage(nc, "outproj", w_outproj)
    w_qkv = nc.dram_tensor("w_qkv", [3072, D], F32, kind="ExternalInput").ap()
    w_lo = nc.dram_tensor("w_lo", [D, 1024], F32, kind="ExternalInput").ap()
    ws_qkv = WStage(nc, "wqkv", w_qkv)
    ws_lo = WStage(nc, "wlo", w_lo)
    zacc_dram = nc.dram_tensor("zacc_dram", [SEQ, D], BF16, kind="Internal").ap()
    xr_pool = nc.dram_tensor("xr_pool", [SEQ, D], BF16, kind="Internal").ap()
    w_rq = nc.dram_tensor("w_rq", [2048, D], F32, kind="ExternalInput").ap()
    w_rkv = nc.dram_tensor("w_rkv", [4096, D], F32, kind="ExternalInput").ap()
    w_rout = nc.dram_tensor("w_rout", [2048, D], F32, kind="ExternalInput").ap()
    ws_wq = WStage(nc, "wrq", w_rq)
    ws_wkv = WStage(nc, "wrkv", w_rkv)
    ws_wout = WStage(nc, "wrout", w_rout)
    kp_dram = nc.dram_tensor("kp_dram", [2, 28, 128, 1024], BF16, kind="Internal").ap()
    vp_dram = nc.dram_tensor("vp_dram", [2, 28, 128, 1024], BF16, kind="Internal").ap()
    x1f_dram = nc.dram_tensor("x1f_dram", [SEQ, D], F32, kind="Internal").ap()
    ag_in = nc.dram_tensor("ag_in", [SEQ, D], BF16, kind="Internal").ap()
    ag_out = nc.dram_tensor("ag_out", [NCORE * SEQ, D], BF16, kind="Internal").ap()
    w_l1qkv = nc.dram_tensor("w_l1qkv", [3072, D], F32, kind="ExternalInput").ap()
    w_m1 = nc.dram_tensor("w_m1", [4096, D], F32, kind="ExternalInput").ap()
    w_m2 = nc.dram_tensor("w_m2", [D, 4096], F32, kind="ExternalInput").ap()
    ws_l1qkv = WStage(nc, "l1qkv", w_l1qkv)
    mask_dram = nc.dram_tensor("mask_dram", [16, 512], BF16, kind="Internal").ap()
    ws_m1 = WStage(nc, "m1", w_m1)
    ws_m2 = WStage(nc, "m2", w_m2)
    ws_extra = {"xr_pool": xr_pool, "ws_wq": ws_wq, "ws_wkv": ws_wkv,
                "ws_wout": ws_wout, "kp_dram": kp_dram, "vp_dram": vp_dram,
                "x1f_dram": x1f_dram, "ag_in": ag_in, "ag_out": ag_out,
                "ws_l1qkv": ws_l1qkv, "ws_m1": ws_m1, "ws_m2": ws_m2,
                "mask_dram": mask_dram}
    ws_extra["pin"] = pin

    with contextlib.ExitStack() as stack:
        sb = lambda n, s, d: stack.enter_context(nc.sbuf_tensor(n, s, d))
        sems = {e: nc.alloc_semaphore("sem_" + e) for e in ENGS}

        X = sb("X", [128, 32 * D], BF16)
        PHASE2_ONLY = ("rope_cos", "rope_sin", "rotPT", "lmask")
        PHASE3_ONLY = ("sel_logk", "sel_kth", "iota_tm", "ish23", "expmask", "nl2b")
        PHASE2_CONSTS = PHASE2_ONLY + PHASE3_ONLY
        PHASE2_PARAMS = ("f_lng", "f_lnb")
        C = {k: sb("c" + k, list(v.shape), F32 if v.dtype == np.float32 else BF16)
             for k, v in consts_np.items() if k not in PHASE2_CONSTS}
        P = {k: sb("p" + k, PARAM_SHAPES[k], F32) for k in PARAM_SHAPES}

        S = Sched()
        for k in consts_np:
            if k in PHASE2_CONSTS:
                continue
            def fn(sy, k=k):
                return sy.dma_start(out=C[k][:], in_=cin[k][:, :])
            fn.dma_n = 1
            S.add("sy", fn, w=(f"c_{k}",), dma=True)
        for k in PARAM_SHAPES:
            def fn(sy, k=k):
                return sy.dma_start(out=P[k][:], in_=pin[k][:, :])
            fn.dma_n = 1
            S.add("sy", fn, w=(f"p_{k}",), dma=True)

        def fn_x(g):
            src = x0.rearrange("(n p) d -> p n d", p=128)
            return g.dma_start(out=X[:].rearrange("p (n d) -> p n d", d=D), in_=src)
        fn_x.dma_n = 1
        S.add("gp", fn_x, w=("X",), dma=True, label="xload")

        if stage >= 1:
            ws_in.emit_casts(S)
            ws_out.emit_casts(S)
            if stage >= 2:
                ws_qkv.emit_casts(S)
                ws_lo.emit_casts(S)
            with contextlib.ExitStack() as st1:
                emit_l0_attn(nc, S, st1, X, C, P, ws_in, ws_out,
                             ws_extra["x1f_dram"])
            S.fence()
        if stage >= 2:
            with contextlib.ExitStack() as st2:
                C2 = dict(C)
                P2 = dict(P)
                for k in PHASE2_ONLY:
                    t = st2.enter_context(nc.sbuf_tensor("c" + k, list(consts_np[k].shape),
                                                         F32 if consts_np[k].dtype == np.float32 else BF16))
                    C2[k] = t
                    def fn(sy, k=k, t=t):
                        return sy.dma_start(out=t[:], in_=cin[k][:, :])
                    fn.dma_n = 1
                    S.add("sy", fn, w=(f"c_{k}",), dma=True)
                for k in PHASE2_PARAMS:
                    t = st2.enter_context(nc.sbuf_tensor("p" + k, PARAM_SHAPES_BF[k], BF16))
                    P2[k] = t
                    def fn(sy, k=k, t=t):
                        return sy.dma_start(out=t[:], in_=pin[k][:, :])
                    fn.dma_n = 1
                    S.add("sy", fn, w=(f"p_{k}",), dma=True)
                emit_l0_ffn_local(nc, S, st2, X, C2, P2, ws_qkv, ws_lo, zacc_dram)
            S.fence()
            if stage >= 3:
                keepstack = stack.pop_all() if False else contextlib.ExitStack()
                ws_extra["member_keep"] = keepstack.enter_context(
                    nc.sbuf_tensor("member_keep", [128, 448], F32))
                ws_extra["recip_keep"] = keepstack.enter_context(
                    nc.sbuf_tensor("recip_keep", [128, 28], F32))
                with contextlib.ExitStack() as st25:
                    C3 = dict(C)
                    for k in ("sel_logk", "sel_kth", "iota_tm", "ish23", "expmask", "nl2b"):
                        t = st25.enter_context(nc.sbuf_tensor("c" + k, list(consts_np[k].shape),
                                               mybir.dt.int32 if k in ("ish23", "expmask") else F32))
                        C3[k] = t
                        def fn(sy, k=k, t=t):
                            return sy.dma_start(out=t[:], in_=cin[k][:, :])
                        fn.dma_n = 1
                        S.add("sy", fn, w=(f"c_{k}",), dma=True)
                    P3 = dict(P)
                    t = st25.enter_context(nc.sbuf_tensor("pf_rtT", PARAM_SHAPES_BF2["f_rtT"], F32))
                    P3["f_rtT"] = t
                    def fn(sy, t=t):
                        return sy.dma_start(out=t[:], in_=pin["f_rtT"][:, :])
                    fn.dma_n = 1
                    S.add("sy", fn, w=("p_f_rtT",), dma=True)
                    emit_l0_ffn_routed(nc, S, st25, X, C3, P3, zacc_dram, stage, ws_extra)
                S.fence()
            if stage >= 4:
                ws_wq.emit_casts(S)
                ws_wkv.emit_casts(S)
                ws_wout.emit_casts(S)
                with contextlib.ExitStack() as st26:
                    emit_l0_ffn_rattn(nc, S, st26, X, C, P, zacc_dram, ws_extra)
                S.fence()
            with contextlib.ExitStack() as st3:
                emit_l0_ffn_combine(nc, S, st3, X, C, P, zacc_dram, stage, ws_extra)
            if stage >= 3:
                keepstack.close()
            S.fence()
        if stage >= 5:
            ws_extra["sems"] = sems
            ws_l1qkv.emit_casts(S)
            with contextlib.ExitStack() as st5:
                emit_l1_attn(nc, S, st5, X, C, P, ws_extra)
            S.fence()
        if stage >= 6:
            ws_m1.emit_casts(S)
            ws_m2.emit_casts(S)
            with contextlib.ExitStack() as st6:
                emit_l1_ffn(nc, S, st6, X, C, P, ws_extra)
            S.fence()

        def fns(g):
            dst = out.rearrange("a b d -> (a b) d").rearrange("(n p) d -> p n d", p=128)
            return g.dma_start(out=dst, in_=X[:].rearrange("p (n d) -> p n d", d=D))
        fns.dma_n = 1
        S.add("gp", fns, r=("X",), dma=True, label="ostore")

        def fing(g):
            return g.engine_nop()
        S.add("gp", fing, after=("ostore",), label="final_fence")

        with nc.Block() as block:
            S.emit(nc, block, sems)

    return nc, consts_np


def emit_l0_attn(nc, S, stack, X, C, P, ws_in, ws_out, x1f_dram=None):
    sb = lambda n, s, d: stack.enter_context(nc.sbuf_tensor(n, s, d))
    psb = lambda n, s, d=F32: stack.enter_context(nc.psum_tensor(n, s, d))
    acs_dram = nc.dram_tensor("acs_dram", [32, TILE], F32, kind="Internal").ap()

    wsl = [sb(f"wslA{j}", [128, 8 * 512], BF16) for j in range(2)]
    wso = sb("wslO", [128, 16 * 512], BF16)
    x_fm = sb("x_fm", [128, 8 * TILE], BF16)
    sg = sb("sg", [128, 16 * TILE], BF16)
    xbc = sb("xbc", [128, 18 * (TILE + 3)], BF16)
    Bfm = sb("Bfm", [128, TILE], BF16)
    Cfm = sb("Cfm", [128, TILE], BF16)
    convt = sb("convt", [128, TILE], F32)
    convs = sb("convs", [128, TILE], BF16)
    dtt = sb("dtt", [32, TILE], F32)
    dta = sb("dta", [32, TILE], F32)
    Acs = sb("Acs", [32, TILE], F32)
    decl = sb("decl", [32, TILE], F32)
    AcsFlat = sb("AcsFlat", [1, 8 * TILE], F32)
    dt_tm = sb("dt_tm", [128, 2 * 32], F32)
    dec_tm = sb("dec_tm", [128, 2 * 32], F32)
    x_tm = sb("x_tm", [128, 2 * M_INTER], BF16)
    Xd_tm = sb("Xd_tm", [128, 2 * M_INTER], BF16)
    Xdd_tm = sb("Xdd_tm", [128, 2 * M_INTER], BF16)
    B_tm = sb("B_tm", [128, 2 * 128], BF16)
    CBm = sb("CBm", [128, 2 * 256], BF16)
    MT = [sb(f"MT{j}", [128, 512], BF16) for j in range(2)]
    Dtmp = [sb(f"Dtmp{j}", [128, 512], F32) for j in range(2)]
    cscE = [sb(f"cscE{j}", [128, 256], BF16) for j in range(2)]
    Csc = [sb(f"Csc{j}", [128, 256], BF16) for j in range(2)]
    Ssm = sb("Ssm", [128, M_HEADS * M_P], BF16)
    cdrow = sb("cdrow", [1, 32], F32)
    cdrep = sb("cdrep", [128, 32], F32)
    Drep = sb("Drep", [128, 32], F32)
    Dident = sb("Dident", [128, 32 * 128], BF16)
    Aneg = sb("Aneg", [32, 1], F32)
    tm32 = sb("tm32", [32, 1], F32)
    y = sb("y", [128, 16 * TILE], BF16)
    ysq = [sb(f"ysq{j}", [128, TILE], BF16) for j in range(2)]
    st_ = sb("st_", [1, 6 * TILE], F32)
    ytmp = sb("ytmp", [128, TILE], F32)
    zta = sb("zta", [128, 2 * D], BF16)
    zsq = sb("zsq", [128, D], BF16)
    zs = sb("zs", [128, 8], F32)

    pT = psb("pT", [128, 512])
    pTb = psb("pTb", [128, 512], BF16)
    pA = psb("pA", [128, 512])
    pB = psb("pB", [128, 512])
    pY = [psb(f"pY{j}", [128, 256]) for j in range(2)]
    pSt = psb("pSt", [128, 64])
    pE = psb("pE", [128, 512])

    idb, idf = C["identity_bf"], C["identity_f32"]
    ones_f, onescol_b = C["ones_f32"], C["onescol_bf"]
    negones = C["negones_f32"]

    def f(s):
        return s.activation(tm32[:], P["m_alog"][:], AF.Exp)
    S.add("se", f, r=("p_m_alog",), w=("tm32",))

    def f(v):
        return v.tensor_scalar_mul(Aneg[:], tm32[:], -1.0)
    S.add("ve", f, r=("tm32",), w=("Aneg",), label="aneg")

    def f(v):
        return v.memset(Ssm[:], 0.0)
    S.add("ve", f, w=tuple(f"Ssm_{h}" for h in range(32)))

    def f(v):
        return v.memset(xbc[:], 0.0)
    S.add("ve", f, w=tuple(f"xbc_{p}" for p in range(18)))

    def f(t):
        return t.matmul(pE[:, 0:32], ones_f[:], P["m_D"][:, :], start=True, stop=True)
    S.add("te", f, r=("c_ones_f32", "p_m_D"), w=("pE",))

    def f(v):
        return v.tensor_copy(Drep[:], pE[:, 0:32])
    S.add("ve", f, r=("pE",), w=("Drep",))

    def f(v):
        ins = None
        for h in range(32):
            ins = v.tensor_scalar(Dident[:, h * 128:(h + 1) * 128], idb[:],
                                  Drep[:, h:h + 1], None, ALU.mult)
        return ins
    S.add("ve", f, r=("c_identity_bf",), sr=("Drep",), w=("Dident",))

    add_dump(nc, S, "tm32", tm32[:], ("tm32",))
    add_dump(nc, S, "Aneg", Aneg[:], ("Aneg",))
    for i in range(NT):
        for dch in range(8):
            def f(t, i=i, dch=dch):
                ins = None
                for ts in range(2):
                    tok = i * 2 + ts
                    ins = t.transpose(pTb[:, ts * 128:(ts + 1) * 128],
                                      X[:, tok * D + dch * 128: tok * D + dch * 128 + 128],
                                      idb[:])
                return ins
            S.add("te", f, r=("X", "c_identity_bf"), w=("pTb",))

            def f(v, dch=dch):
                return v.tensor_copy(x_fm[:, dch * TILE:(dch + 1) * TILE], pTb[:, 0:256])
            S.add("ve", f, r=("pTb",), w=("x_fm",))
        for g_oc in range(35):
            sl, oc = g_oc // 4, g_oc % 4
            if oc == 0:
                ws_in.load_slice(S, "sy", wsl[sl % 2][:], sl, f"wslA{sl % 2}",
                                 label=f"wA_{i}_{sl}")
            ocw = 32 if g_oc == 34 else 128
            buf = wsl[sl % 2]
            pb, pbk = (pA, "pA") if g_oc % 2 == 0 else (pB, "pB")

            def f(t, oc=oc, ocw=ocw, buf=buf, pb=pb):
                ins = None
                for ch in range(8):
                    ins = t.matmul(pb[0:ocw, 0:TILE],
                                   buf[:, ch * 512 + oc * 128: ch * 512 + oc * 128 + ocw],
                                   x_fm[:, ch * TILE:(ch + 1) * TILE],
                                   start=(ch == 0), stop=(ch == 7))
                return ins
            S.add("te", f, r=(f"wslA{sl % 2}", "x_fm"), w=(pbk,), label=f"pj_{i}_{g_oc}")
            if g_oc < 16:
                def f(s, g_oc=g_oc, pb=pb):
                    return s.activation(sg[:, g_oc * TILE:(g_oc + 1) * TILE],
                                        pb[0:128, 0:TILE], AF.Silu)
                S.add("se", f, r=(pbk,), w=("sg",))
            elif g_oc < 34:
                xch = g_oc - 16

                def f(v, xch=xch, pb=pb):
                    return v.tensor_copy(
                        xbc[:, xch * (TILE + 3) + 3:(xch + 1) * (TILE + 3)],
                        pb[0:128, 0:TILE])
                S.add("ve", f, r=(pbk,), w=(f"xbc_{xch}",))
            else:
                def f(v, pb=pb):
                    return v.tensor_scalar(dtt[:, :], pb[0:32, 0:TILE],
                                           P["m_dtbias"][:], None, ALU.add)
                S.add("ve", f, r=(pbk,), sr=("p_m_dtbias",), w=("dtt",))
        if i == DUMP_TILE:
            add_dump(nc, S, "sg", sg[:], ("sg",))
            add_dump(nc, S, "dtraw", dtt[:], ("dtt",))
        for pch in range(18):
            def f(v, pch=pch):
                ins = None
                for kk in range(4):
                    src = xbc[:, pch * (TILE + 3) + kk: pch * (TILE + 3) + kk + TILE]
                    wk = P["m_convw"][:, pch * 4 + kk: pch * 4 + kk + 1]
                    if kk == 0:
                        ins = v.tensor_scalar(convt[:], src, wk, None, ALU.mult)
                    else:
                        ins = v.scalar_tensor_tensor(convt[:], src, wk, convt[:],
                                                     ALU.mult, ALU.add)
                return ins
            S.add("ve", f, r=(f"xbc_{pch}",), sr=("p_m_convw",), w=("convt",))

            def f(s, pch=pch):
                return s.activation(convs[:], convt[:], AF.Silu,
                                    bias=P["m_convb"][:, pch:pch + 1])
            S.add("se", f, r=("convt",), sr=("p_m_convb",), w=("convs",))

            def f(v, pch=pch):
                return v.tensor_copy(xbc[:, pch * (TILE + 3): pch * (TILE + 3) + 3],
                                     xbc[:, pch * (TILE + 3) + TILE:(pch + 1) * (TILE + 3)])
            S.add("ve", f, r=(f"xbc_{pch}",), w=(f"xbc_{pch}",))
            if pch < 16:
                def f(t, pch=pch):
                    ins = None
                    for half in range(2):
                        ins = t.transpose(pTb[:, half * 128:(half + 1) * 128],
                                          convs[:, half * 128:(half + 1) * 128], idb[:])
                    return ins
                S.add("te", f, r=("convs", "c_identity_bf"), w=("pTb",))

                def f(v, pch=pch):
                    ins = None
                    for half in range(2):
                        ins = v.tensor_copy(
                            x_tm[:, half * M_INTER + pch * 128: half * M_INTER + (pch + 1) * 128],
                            pTb[:, half * 128:(half + 1) * 128])
                    return ins
                S.add("ve", f, r=("pTb",), w=("x_tm",))
            elif pch == 16:
                def f(v):
                    return v.tensor_copy(Bfm[:], convs[:])
                S.add("ve", f, r=("convs",), w=("Bfm",))
            else:
                def f(v):
                    return v.tensor_copy(Cfm[:], convs[:])
                S.add("ve", f, r=("convs",), w=("Cfm",))
        if i == DUMP_TILE:
            add_dump(nc, S, "Bfm", Bfm[:], ("Bfm",))
            add_dump(nc, S, "Cfm", Cfm[:], ("Cfm",))
            add_dump(nc, S, "x_tm", x_tm[:], ("x_tm",))
        def f(s):
            return s.activation(dta[:], dtt[:], AF.Abs)
        S.add("se", f, r=("dtt",), w=("dta",))

        def f(s):
            return s.activation(dta[:], dta[:], AF.Exp, scale=-1.0)
        S.add("se", f, r=("dta",), w=("dta",))

        def f(s):
            return s.activation(dta[:], dta[:], AF.Ln, bias=1.0)
        S.add("se", f, r=("dta",), w=("dta",))

        def f(v):
            return v.tensor_scalar(dtt[:], dtt[:], 0.0, None, ALU.max)
        S.add("ve", f, r=("dtt",), w=("dtt",))

        def f(v):
            return v.tensor_add(dtt[:], dtt[:], dta[:])
        S.add("ve", f, r=("dta", "dtt"), w=("dtt",))

        def f(v):
            return v.tensor_scalar(dta[:], dtt[:], Aneg[:], None, ALU.mult)
        S.add("ve", f, r=("dtt",), sr=("Aneg",), w=("dta",))

        def f(v):
            return v.tensor_tensor_scan(Acs[:], dta[:], dta[:], 0.0, ALU.add, ALU.bypass)
        S.add("ve", f, r=("dta",), w=("Acs",))

        def f(v):
            return v.tensor_scalar(decl[:], Acs[:], Acs[:, TILE - 1:TILE], -1.0,
                                   ALU.subtract, ALU.mult)
        S.add("ve", f, r=("Acs",), sr=("Acs",), w=("decl",))

        def f(s):
            return s.activation(decl[:], decl[:], AF.Exp)
        S.add("se", f, r=("decl",), w=("decl",))

        if i == DUMP_TILE:
            add_dump(nc, S, "dta", dta[:], ("dta",))
            add_dump(nc, S, "dt", dtt[:], ("dtt",))
            add_dump(nc, S, "Acs", Acs[:], ("Acs",))
            add_dump(nc, S, "dec", decl[:], ("decl",))
        def f(t):
            ins = None
            for half in range(2):
                t.transpose(pT[:, half * 64: half * 64 + 32],
                            dtt[:, half * 128:(half + 1) * 128], idf[0:32, 0:32])
                ins = t.transpose(pT[:, half * 64 + 32: half * 64 + 64],
                                  decl[:, half * 128:(half + 1) * 128], idf[0:32, 0:32])
            return ins
        S.add("te", f, r=("dtt", "decl", "c_identity_f32"), w=("pT",))

        def f(v):
            ins = None
            for half in range(2):
                v.tensor_copy(dt_tm[:, half * 32:(half + 1) * 32],
                              pT[:, half * 64: half * 64 + 32])
                ins = v.tensor_copy(dec_tm[:, half * 32:(half + 1) * 32],
                                    pT[:, half * 64 + 32: half * 64 + 64])
            return ins
        S.add("ve", f, r=("pT",), w=("dt_tm", "dec_tm"))

        def f(t):
            ins = None
            for half in range(2):
                ins = t.transpose(pTb[:, half * 128:(half + 1) * 128],
                                  Bfm[:, half * 128:(half + 1) * 128], idb[:])
            return ins
        S.add("te", f, r=("Bfm", "c_identity_bf"), w=("pTb",))

        def f(v):
            return v.tensor_copy(B_tm[:], pTb[:, 0:256])
        S.add("ve", f, r=("pTb",), w=("B_tm",))

        def f(v):
            ins = None
            for half in range(2):
                dtb = dt_tm[:, half * 32:(half + 1) * 32]
                ins = v.tensor_tensor(
                    Xd_tm[:, half * M_INTER:(half + 1) * M_INTER],
                    x_tm[:, half * M_INTER:(half + 1) * M_INTER],
                    dtb.unsqueeze(2).broadcast_to([128, 32, 64]),
                    ALU.mult)
            return ins
        S.add("ve", f, r=("x_tm", "dt_tm"), w=("Xd_tm",))

        def f(v):
            ins = None
            for half in range(2):
                dcb = dec_tm[:, half * 32:(half + 1) * 32]
                ins = v.tensor_tensor(
                    Xdd_tm[:, half * M_INTER:(half + 1) * M_INTER],
                    Xd_tm[:, half * M_INTER:(half + 1) * M_INTER],
                    dcb.unsqueeze(2).broadcast_to([128, 32, 64]),
                    ALU.mult)
            return ins
        S.add("ve", f, r=("Xd_tm", "dec_tm"), w=("Xdd_tm",))

        def f(t):
            ins = None
            for half in range(2):
                ins = t.matmul(pT[:, half * 256:(half + 1) * 256],
                               Bfm[:, half * 128:(half + 1) * 128],
                               Cfm[:, 0:256], start=True, stop=True)
            return ins
        S.add("te", f, r=("Bfm", "Cfm"), w=("pT",))

        def f(v):
            ins = None
            for half in range(2):
                ins = v.tensor_tensor(CBm[:, half * 256:(half + 1) * 256],
                                      pT[:, half * 256:(half + 1) * 256],
                                      C["ssd_m%d" % half][:], ALU.mult)
            return ins
        S.add("ve", f, r=("pT", "c_ssd_m0", "c_ssd_m1"), w=("CBm",))

        def f(g):
            return g.dma_start(out=acs_dram[:, :], in_=Acs[:])
        f.dma_n = 1
        S.add("gp", f, r=("Acs",), w=("acsdram",), dma=True)
        for q in range(4):
            def f(g, q=q):
                src = acs_dram[q * 8:(q + 1) * 8, :].rearrange("h t -> (h t)")
                return g.dma_start(out=AcsFlat[:, :], in_=src.unsqueeze(0))
            f.dma_n = 1
            S.add("gp", f, r=("acsdram",), w=("AcsFlat",), dma=True)

            def f(s, q=q):
                cds = AcsFlat[:].rearrange("o (h t) -> o h t", t=TILE)[:, :, TILE - 1]
                return s.activation(cdrow[:, q * 8:(q + 1) * 8], cds, AF.Exp)
            S.add("se", f, r=("AcsFlat",), w=("cdrow",))

            def f(t, q=q):
                return t.matmul(pE[:, 448:456], ones_f[:], cdrow[:, q * 8:(q + 1) * 8],
                                start=True, stop=True)
            S.add("te", f, r=("cdrow", "c_ones_f32"), w=("pE",))

            def f(v, q=q):
                return v.tensor_copy(cdrep[:, q * 8:(q + 1) * 8], pE[:, 448:456])
            S.add("ve", f, r=("pE",), w=("cdrep",))

            for hh in range(8):
                h = q * 8 + hh
                par = h % 2
                pDb, pDk = (pA, "pA") if par == 0 else (pB, "pB")
                arow = AcsFlat[:, hh * TILE:(hh + 1) * TILE]

                def f(t, arow=arow, pDb=pDb):
                    ins = None
                    for tj in range(2):
                        t.matmul(pDb[:, tj * 256:(tj + 1) * 256], ones_f[:],
                                 arow, start=True, stop=False)
                        ins = t.matmul(pDb[:, tj * 256:(tj + 1) * 256],
                                       arow[:, tj * 128:(tj + 1) * 128],
                                       negones[:, 0:256], start=False, stop=True)
                    return ins
                S.add("te", f, r=("AcsFlat", "c_ones_f32", "c_negones_f32"), w=(pDk,))

                def f(v, par=par, pDb=pDb):
                    return v.tensor_single_scalar(Dtmp[par][:], pDb[:, 0:512], 0.0,
                                                  ALU.min)
                S.add("ve", f, r=(pDk,), w=(f"Dtmp{par}",))

                def f(s, par=par):
                    return s.activation(Dtmp[par][:], Dtmp[par][:], AF.Exp)
                S.add("se", f, r=(f"Dtmp{par}",), w=(f"Dtmp{par}",))

                def f(v, par=par):
                    return v.tensor_tensor(MT[par][:], Dtmp[par][:], CBm[:], ALU.mult)
                S.add("ve", f, r=(f"Dtmp{par}", "CBm"), w=(f"MT{par}",))

                def f(t, arow=arow):
                    return t.matmul(pE[:, 0:256], ones_f[:], arow, start=True, stop=True)
                S.add("te", f, r=("AcsFlat", "c_ones_f32"), w=("pE",))

                def f(s, par=par):
                    return s.activation(cscE[par][:], pE[:, 0:256], AF.Exp)
                S.add("se", f, r=("pE",), w=(f"cscE{par}",))

                def f(v, par=par):
                    return v.tensor_tensor(Csc[par][:], cscE[par][:], Cfm[:, 0:256],
                                           ALU.mult)
                S.add("ve", f, r=(f"cscE{par}", "Cfm"), w=(f"Csc{par}",))

                pyk = f"pY{(h // 2) % 2}"
                pyb = pY[(h // 2) % 2]
                odd = h % 2
                tp = (0, 64) if odd else None

                def f(t, h=h, par=par, pyb=pyb, odd=odd, tp=tp):
                    p0 = odd * 64
                    for tj in range(2):
                        t.matmul(pyb[p0:p0 + 64, :],
                                 Xd_tm[:, tj * M_INTER + h * 64: tj * M_INTER + h * 64 + 64],
                                 MT[par][:, tj * 256:(tj + 1) * 256],
                                 start=(tj == 0), stop=False, tile_position=tp)
                    t.matmul(pyb[p0:p0 + 64, :], Ssm[:, h * 64:(h + 1) * 64],
                             Csc[par][:], start=False, stop=False, tile_position=tp)
                    ins = None
                    for tj in range(2):
                        ins = t.matmul(pyb[p0:p0 + 64, tj * 128:(tj + 1) * 128],
                                       x_tm[:, tj * M_INTER + h * 64: tj * M_INTER + h * 64 + 64],
                                       Dident[:, h * 128:(h + 1) * 128],
                                       start=False, stop=(tj == 1), tile_position=tp)
                    return ins
                S.add("te", f, r=("Xd_tm", f"MT{par}", f"Ssm_{h}", f"Csc{par}",
                                  "x_tm", "Dident"), w=(pyk,), label=f"yg_{i}_{h}")

                def f(t, h=h):
                    ins = None
                    for tj in range(2):
                        ins = t.matmul(pSt[:, 0:64],
                                       B_tm[:, tj * 128:(tj + 1) * 128],
                                       Xdd_tm[:, tj * M_INTER + h * 64: tj * M_INTER + h * 64 + 64],
                                       start=(tj == 0), stop=(tj == 1))
                    return ins
                S.add("te", f, r=("B_tm", "Xdd_tm"), w=("pSt",))

                def f(v, h=h):
                    return v.scalar_tensor_tensor(Ssm[:, h * 64:(h + 1) * 64],
                                                  Ssm[:, h * 64:(h + 1) * 64],
                                                  cdrep[:, h:h + 1], pSt[:, 0:64],
                                                  ALU.mult, ALU.add)
                S.add("ve", f, r=("pSt",), sr=("cdrep",), w=(f"Ssm_{h}",),
                      after=(f"yg_{i}_{h}",))

                if odd:
                    qq = h // 2

                    def f(v, qq=qq, pyb=pyb):
                        return v.tensor_copy(y[:, qq * TILE:(qq + 1) * TILE], pyb[:, :])
                    S.add("ve", f, r=(pyk,), w=(f"y_{qq}",))

        if i == DUMP_TILE:
            add_dump(nc, S, "ypre", y[:], tuple(f"y_{ch}" for ch in range(16)))
            add_dump(nc, S, "Ssm", Ssm[:], tuple(f"Ssm_{h}" for h in range(32)))
        def f(t):
            ins = None
            for ch in range(16):
                ins = t.matmul(pT[0:1, 0:TILE], onescol_b[:],
                               y[:, ch * TILE:(ch + 1) * TILE],
                               start=(ch == 0), stop=(ch == 15))
            return ins
        S.add("te", f, r=tuple(f"y_{ch}" for ch in range(16)) + ("c_onescol_bf",),
              w=("pT",))

        def f(v):
            return v.tensor_copy(st_[:, 0:TILE], pT[0:1, 0:TILE])
        S.add("ve", f, r=("pT",), w=("st_",))

        for ch in range(16):
            def f(s, ch=ch):
                return s.activation(ysq[ch % 2][:], y[:, ch * TILE:(ch + 1) * TILE],
                                    AF.Square)
            S.add("se", f, r=(f"y_{ch}",), w=(f"ysq{ch % 2}",), label=f"ysq_{i}_{ch}")

            def f(t, ch=ch):
                return t.matmul(pE[0:1, 0:TILE], onescol_b[:], ysq[ch % 2][:],
                                start=(ch == 0), stop=(ch == 15))
            S.add("te", f, r=(f"ysq{ch % 2}", "c_onescol_bf"),
                  w=("pE",) if ch in (0, 15) else (),
                  label=f"ysq2_{i}_{ch}")

        def f(v):
            return v.tensor_copy(st_[:, TILE:2 * TILE], pE[0:1, 0:TILE])
        S.add("ve", f, r=("pE",), w=("st_",))

        def f(v):
            return v.tensor_scalar_mul(st_[:, 2 * TILE:3 * TILE], st_[:, 0:TILE],
                                       1.0 / M_INTER)
        S.add("ve", f, r=("st_",), w=("st_mu",))

        def f(s):
            return s.activation(st_[:, 5 * TILE:6 * TILE], st_[:, 2 * TILE:3 * TILE],
                                AF.Square)
        S.add("se", f, r=("st_mu",), w=("st_tmp",))

        def f(v):
            return v.scalar_tensor_tensor(st_[:, 5 * TILE:6 * TILE], st_[:, TILE:2 * TILE],
                                          1.0 / M_INTER, st_[:, 5 * TILE:6 * TILE],
                                          ALU.mult, ALU.subtract)
        S.add("ve", f, r=("st_", "st_tmp"), w=("st_var",))

        def f(s):
            return s.activation(st_[:, 3 * TILE:4 * TILE], st_[:, 5 * TILE:6 * TILE],
                                AF.Ln, bias=C["eps_ln"][0:1, :])
        S.add("se", f, r=("st_var",), sr=("c_eps_ln",), w=("st_rs0",))

        def f(s):
            return s.activation(st_[:, 3 * TILE:4 * TILE], st_[:, 3 * TILE:4 * TILE],
                                AF.Exp, scale=-0.5)
        S.add("se", f, r=("st_rs0",), w=("st_rs",))

        def f(v):
            return v.tensor_mul(st_[:, 4 * TILE:5 * TILE], st_[:, 2 * TILE:3 * TILE],
                                st_[:, 3 * TILE:4 * TILE])
        S.add("ve", f, r=("st_mu", "st_rs"), w=("st_murs",))

        def f(t):
            t.matmul(pA[:, 0:TILE], ones_f[:], st_[:, 3 * TILE:4 * TILE],
                     start=True, stop=True)
            return t.matmul(pB[:, 0:TILE], ones_f[:], st_[:, 4 * TILE:5 * TILE],
                            start=True, stop=True)
        S.add("te", f, r=("st_rs", "st_murs", "c_ones_f32"), w=("pA", "pB"),
              label=f"lnexp_{i}")

        for ch in range(16):
            def f(v, ch=ch):
                v.tensor_tensor(ytmp[:], y[:, ch * TILE:(ch + 1) * TILE], pA[:, 0:TILE],
                                ALU.mult)
                v.tensor_tensor(ytmp[:], ytmp[:], pB[:, 0:TILE], ALU.subtract)
                v.tensor_scalar(ytmp[:], ytmp[:], P["m_lnw"][:, ch:ch + 1],
                                P["m_lnb"][:, ch:ch + 1], ALU.mult, ALU.add)
                return v.tensor_tensor(y[:, ch * TILE:(ch + 1) * TILE], ytmp[:],
                                       sg[:, ch * TILE:(ch + 1) * TILE], ALU.mult)
            S.add("ve", f, r=("pA", "pB", f"y_{ch}", "sg"), sr=("p_m_lnw", "p_m_lnb"),
                  w=(f"y_{ch}",), label=f"ynsg_{i}_{ch}")

        if i == DUMP_TILE:
            add_dump(nc, S, "ynsg", y[:], tuple(f"y_{ch}" for ch in range(16)))
        for sl in range(2):
            ws_out.load_slice(S, "se", wso[:], sl, "wslO", label=f"wO_{i}_{sl}")
            for ts in range(2):
                tok = i * 2 + ts
                pb, pbk = (pA, "pA") if ts == 0 else (pB, "pB")

                def f(t, ts=ts, pb=pb):
                    ins = None
                    for ch in range(16):
                        ins = t.matmul(pb[:, 0:512],
                                       y[:, ch * TILE + ts * 128: ch * TILE + ts * 128 + 128],
                                       wso[:, ch * 512:(ch + 1) * 512],
                                       start=(ch == 0), stop=(ch == 15))
                    return ins
                S.add("te", f, r=tuple(f"y_{ch}" for ch in range(16)) + ("wslO",),
                      w=(pbk,), label=f"op_{i}_{ts}_{sl}")

                def f(v, ts=ts, sl=sl, pb=pb, tok=tok):
                    return v.tensor_tensor(zta[:, ts * D + sl * 512: ts * D + (sl + 1) * 512],
                                           pb[:, 0:512],
                                           X[:, tok * D + sl * 512: tok * D + (sl + 1) * 512],
                                           ALU.add)
                S.add("ve", f, r=(pbk, "X"), w=(f"zta_{ts}",))
        if i == DUMP_TILE:
            add_dump(nc, S, "zta", zta[:], ("zta_0", "zta_1"))
        for ts in range(2):
            tok = i * 2 + ts
            def f(v, ts=ts):
                return v.tensor_reduce(zs[:, 2:3], zta[:, ts * D:(ts + 1) * D], AX.X,
                                       ALU.add)
            S.add("ve", f, r=(f"zta_{ts}",), w=("zs_s1",))

            def f(s, ts=ts):
                return s.activation(zsq[:], zta[:, ts * D:(ts + 1) * D], AF.Square)
            S.add("se", f, r=(f"zta_{ts}",), w=("zsq",))

            def f(v):
                return v.tensor_reduce(zs[:, 3:4], zsq[:], AX.X, ALU.add)
            S.add("ve", f, r=("zsq",), w=("zs_s2",))

            def f(v):
                return v.tensor_scalar_mul(zs[:, 0:1], zs[:, 2:3], 1.0 / D)
            S.add("ve", f, r=("zs_s1",), w=("zs_mu",))

            def f(s):
                return s.activation(zs[:, 4:5], zs[:, 0:1], AF.Square)
            S.add("se", f, r=("zs_mu",), w=("zs_musq",))

            def f(v):
                return v.scalar_tensor_tensor(zs[:, 5:6], zs[:, 3:4], 1.0 / D,
                                              zs[:, 4:5], ALU.mult, ALU.subtract)
            S.add("ve", f, r=("zs_s2", "zs_musq"), w=("zs_var",))

            def f(s):
                return s.activation(zs[:, 1:2], zs[:, 5:6], AF.Ln, bias=C["eps_ln"][:, :])
            S.add("se", f, r=("zs_var",), sr=("c_eps_ln",), w=("zs_rs0",))

            def f(s):
                return s.activation(zs[:, 1:2], zs[:, 1:2], AF.Exp, scale=-0.5)
            S.add("se", f, r=("zs_rs0",), w=("zs_rs",))

            def f(v, ts=ts, tok=tok):
                return v.tensor_scalar(X[:, tok * D:(tok + 1) * D],
                                       zta[:, ts * D:(ts + 1) * D],
                                       zs[:, 0:1], zs[:, 1:2], ALU.subtract, ALU.mult)
            S.add("ve", f, r=(f"zta_{ts}",), sr=("zs_mu", "zs_rs"), w=("X",),
                  label=f"xnew_{i}_{ts}")
            for hf in range(2):
                def f(v, ts=ts, hf=hf):
                    return v.tensor_scalar(
                        Dtmp[hf][:, 0:512],
                        zta[:, ts * D + hf * 512: ts * D + hf * 512 + 512],
                        zs[:, 0:1], zs[:, 1:2], ALU.subtract, ALU.mult)
                S.add("ve", f, r=(f"zta_{ts}",), sr=("zs_mu", "zs_rs"),
                      w=(f"Dtmp{hf}",))

                def f(g, tok=tok, hf=hf):
                    dst = x1f_dram.rearrange("(n p) d -> n p d", p=128)[tok][
                        :, hf * 512:hf * 512 + 512]
                    return g.dma_start(out=dst, in_=Dtmp[hf][:, 0:512])
                f.dma_n = 1
                S.add("gp", f, r=(f"Dtmp{hf}",), w=(f"x1f_{tok}_{hf}",), dma=True)


def kernel(**inputs):
    nc, consts_np = build(stage=KSTAGE)
    x = np.ascontiguousarray(np.asarray(inputs["L0_attn_hidden_states"]))
    pp = prep_params(inputs)
    base = {}
    base.update({"c_" + k: v for k, v in consts_np.items()})
    base.update(pp)
    base["w_inproj"] = np.ascontiguousarray(np.asarray(inputs["L0_attn_in_proj_w"]))
    base["w_outproj"] = np.ascontiguousarray(np.asarray(inputs["L0_attn_out_proj_w"]))
    base["w_qkv"] = np.ascontiguousarray(np.asarray(inputs["L0_ffn_w_qkv"]))
    base["w_lo"] = np.ascontiguousarray(np.asarray(inputs["L0_ffn_w_lo"]))
    if True:
        base["w_rq"] = np.ascontiguousarray(
            np.asarray(inputs["L0_ffn_w_q"]).reshape(2048, 1024))
        base["w_rkv"] = np.ascontiguousarray(
            np.asarray(inputs["L0_ffn_w_kv"]).reshape(4096, 1024))
        base["w_rout"] = np.ascontiguousarray(
            np.asarray(inputs["L0_ffn_w_out"]).reshape(2048, 1024))
        perm = np.concatenate(
            [(np.arange(16)[:, None] * 192 + a * 64 + np.arange(64)[None, :])
             for a in range(3)], axis=0).reshape(-1)
        base["w_l1qkv"] = np.ascontiguousarray(
            np.asarray(inputs["L1_attn_W_qkv"])[perm])
        base["w_m1"] = np.ascontiguousarray(np.asarray(inputs["L1_ffn_w1"]))
        base["w_m2"] = np.ascontiguousarray(np.asarray(inputs["L1_ffn_w2"]))
    in_maps = []
    for ci in range(NCORE):
        m = dict(base)
        m["x0"] = x[ci]
        in_maps.append(m)
    res = run_bass_kernel_spmd(nc, in_maps, core_ids=list(range(NCORE)))
    return np.concatenate([r["out"] for r in res.results], axis=1)




def emit_l0_ffn_local(nc, S, stack, X, C, P, ws_qkv, ws_lo, zacc_dram):
    sb = lambda n, s, d: stack.enter_context(nc.sbuf_tensor(n, s, d))
    psb = lambda n, s, d=F32: stack.enter_context(nc.psum_tensor(n, s, d))

    wsl = [sb(f"lw{j}", [128, 8 * 512], BF16) for j in range(2)]
    wlo_s = sb("wlo_s", [128, 8 * 512], BF16)
    xn_tm = sb("xn_tm", [128, 4 * D], BF16)
    xn_fm = sb("xn_fm", [128, 8 * 512], BF16)
    q_fm = sb("q_fm", [128, 8 * 512], BF16)
    k_fm = [sb(f"k_fm{j}", [128, 8 * 512], BF16) for j in range(2)]
    v_tm = [sb(f"v_tm{j}", [128, 4 * D], BF16) for j in range(2)]
    PT = [sb(f"lPT{j}", [128, 1024], BF16) for j in range(2)]
    o_fm = sb("o_fm", [128, 8 * 512], BF16)
    rsum = sb("rsum", [1, 1024], F32)
    rexp = sb("rexp", [128, 512], BF16)
    rt1 = sb("rt1", [128, 512], BF16)
    rt2 = sb("rt2", [128, 512], BF16)
    lzs = sb("lzs", [128, 8], F32)
    ltmp = sb("ltmp", [128, D], BF16)
    lst = [sb(f"lst{j}", [128, D], BF16) for j in range(4)]

    pA = psb("lpA", [128, 512])
    pB = psb("lpB", [128, 512])
    pS2 = psb("lpS2", [128, 1024])
    pTb2 = psb("lpTb2", [128, 512], BF16)
    pO = [psb(f"lpO{j}", [128, 512]) for j in range(2)]

    ones_f = C["ones_f32"]
    onescol_b = C["onescol_bf"]
    idb = C["identity_bf"]

    for w in range(8):
        cw = w % 2
        for tt in range(4):
            tok = 4 * w + tt
            Xt = X[:, tok * D:(tok + 1) * D]

            def f(v, Xt=Xt):
                return v.tensor_reduce(lzs[:, 2:3], Xt, AX.X, ALU.add)
            S.add("ve", f, r=("X",), w=("lzs_s1",))

            def f(s, Xt=Xt):
                return s.activation(ltmp[:], Xt, AF.Square)
            S.add("se", f, r=("X",), w=("ltmp",))

            def f(v):
                return v.tensor_reduce(lzs[:, 3:4], ltmp[:], AX.X, ALU.add)
            S.add("ve", f, r=("ltmp",), w=("lzs_s2",))

            def f(v):
                return v.tensor_scalar_mul(lzs[:, 0:1], lzs[:, 2:3], 1.0 / D)
            S.add("ve", f, r=("lzs_s1",), w=("lzs_mu",))

            def f(s):
                return s.activation(lzs[:, 4:5], lzs[:, 0:1], AF.Square)
            S.add("se", f, r=("lzs_mu",), w=("lzs_musq",))

            def f(v):
                return v.scalar_tensor_tensor(lzs[:, 5:6], lzs[:, 3:4], 1.0 / D,
                                              lzs[:, 4:5], ALU.mult, ALU.subtract)
            S.add("ve", f, r=("lzs_s2", "lzs_musq"), w=("lzs_var",))

            def f(s):
                return s.activation(lzs[:, 1:2], lzs[:, 5:6], AF.Ln, bias=C["eps_ln"][:, :])
            S.add("se", f, r=("lzs_var",), sr=("c_eps_ln",), w=("lzs_rs0",))

            def f(s):
                return s.activation(lzs[:, 1:2], lzs[:, 1:2], AF.Exp, scale=-0.5)
            S.add("se", f, r=("lzs_rs0",), w=("lzs_rs",))

            def f(v, Xt=Xt, tt=tt):
                return v.tensor_scalar(ltmp[:], Xt, lzs[:, 0:1], lzs[:, 1:2],
                                       ALU.subtract, ALU.mult)
            S.add("ve", f, r=("X",), sr=("lzs_mu", "lzs_rs"), w=("ltmp",))

            def f(v, tt=tt):
                v.tensor_tensor(ltmp[:], ltmp[:], P["f_lng"][:], ALU.mult)
                return v.tensor_tensor(xn_tm[:, tt * D:(tt + 1) * D], ltmp[:],
                                       P["f_lnb"][:], ALU.add)
            S.add("ve", f, r=("ltmp", "p_f_lng", "p_f_lnb"), w=("xn_tm",),
                  label=f"lxn_{w}_{tt}")
        for dch in range(8):
            def f(t, dch=dch):
                ins = None
                for tt in range(4):
                    ins = t.transpose(pTb2[:, tt * 128:(tt + 1) * 128],
                                     xn_tm[:, tt * D + dch * 128: tt * D + dch * 128 + 128],
                                     idb[:])
                return ins
            S.add("te", f, r=("xn_tm", "c_identity_bf"), w=("lpTb2",))

            def f(v, dch=dch):
                return v.tensor_copy(xn_fm[:, dch * 512:(dch + 1) * 512], pTb2[:, 0:512])
            S.add("ve", f, r=("lpTb2",), w=("xn_fm",))
        for sl in range(6):
            buf = wsl[sl % 2]
            ws_qkv.load_slice(S, "sy", buf[:], sl, f"lw{sl % 2}", label=f"lwA_{w}_{sl}")
            if sl < 4:
                for oc in range(4):
                    goc = (sl % 2) * 4 + oc
                    pb, pbk = (pA, "lpA") if oc % 2 == 0 else (pB, "lpB")

                    def f(t, oc=oc, buf=buf, pb=pb):
                        ins = None
                        for ch in range(8):
                            ins = t.matmul(pb[:, 0:512],
                                           buf[:, ch * 512 + oc * 128: ch * 512 + oc * 128 + 128],
                                           xn_fm[:, ch * 512:(ch + 1) * 512],
                                           start=(ch == 0), stop=(ch == 7))
                        return ins
                    S.add("te", f, r=(f"lw{sl % 2}", "xn_fm"), w=(pbk,))

                    if sl < 2:
                        def f(v, goc=goc, pb=pb):
                            return v.tensor_copy(q_fm[:, goc * 512:(goc + 1) * 512],
                                                 pb[:, 0:512])
                        S.add("ve", f, r=(pbk,), w=("q_fm",))
                    else:
                        def f(v, goc=goc, pb=pb, cw=cw):
                            return v.tensor_copy(k_fm[cw][:, goc * 512:(goc + 1) * 512],
                                                 pb[:, 0:512])
                        S.add("ve", f, r=(pbk,), w=(f"k_fm{cw}",))
            else:
                for tt in range(4):
                    pb, pbk = (pA, "lpA") if tt % 2 == 0 else (pB, "lpB")

                    def f(t, tt=tt, buf=buf, pb=pb):
                        ins = None
                        for ch in range(8):
                            ins = t.matmul(pb[:, 0:512],
                                           xn_fm[:, ch * 512 + tt * 128: ch * 512 + tt * 128 + 128],
                                           buf[:, ch * 512:(ch + 1) * 512],
                                           start=(ch == 0), stop=(ch == 7))
                        return ins
                    S.add("te", f, r=(f"lw{sl % 2}", "xn_fm"), w=(pbk,))

                    def f(v, tt=tt, sl=sl, pb=pb, cw=cw):
                        return v.tensor_copy(
                            v_tm[cw][:, tt * D + (sl - 4) * 512: tt * D + (sl - 4) * 512 + 512],
                            pb[:, 0:512])
                    S.add("ve", f, r=(pbk,), w=(f"v_tm{cw}",))
        if w == 0:
            add_dump(nc, S, "l_xnfm", xn_fm[:], ("xn_fm",))
        for which, buf, bkey in (("q", q_fm, "q_fm"), ("k", k_fm[cw], f"k_fm{cw}")):
            for ch in range(8):
                def f(t, buf=buf, ch=ch):
                    return t.matmul(pA[:, 0:512], C["rotPT"][:],
                                    buf[:, ch * 512:(ch + 1) * 512], start=True, stop=True)
                S.add("te", f, r=(bkey, "c_rotPT"), w=("lpA",))

                def f(v, buf=buf, ch=ch, w_=w):
                    cs = C["rope_cos"][:, 512 * w_: 512 * w_ + 512]
                    sn = C["rope_sin"][:, 512 * w_: 512 * w_ + 512]
                    v.tensor_tensor(rt1[:], buf[:, ch * 512:(ch + 1) * 512], cs, ALU.mult)
                    v.tensor_tensor(rt2[:], pA[:, 0:512], sn, ALU.mult)
                    return v.tensor_add(buf[:, ch * 512:(ch + 1) * 512], rt1[:], rt2[:])
                S.add("ve", f, r=("lpA", bkey, "c_rope_cos", "c_rope_sin"),
                      w=(bkey, "rt1", "rt2"), label=f"rope_{w}_{which}_{ch}")
        if w == 0:
            add_dump(nc, S, "l_qfm", q_fm[:], tuple(f"rope_{0}_q_{c}" for c in range(8)))
            add_dump(nc, S, "l_kfm", k_fm[0][:], tuple(f"rope_{0}_k_{c}" for c in range(8)))
            add_dump(nc, S, "l_vtm", v_tm[0][:], ("v_tm0",))
        for h in range(16):
            hc, hp = h // 2, h % 2
            pq = hc % 2
            p0 = hp * 64
            tp = (0, 64) if hp else None
            kts = list(range(4, 8)) if w == 0 else list(range(8))
            qsl = q_fm[p0:p0 + 64, hc * 512: hc * 512 + 512]
            first = True
            for bi in range(0, len(kts), 2):
                pair = kts[bi:bi + 2]
                par = (bi // 2) % 2
                for j, kt in enumerate(pair):
                    win = cw if kt >= 4 else 1 - cw
                    ktl = kt % 4

                    def f(t, kt=kt, j=j, win=win, ktl=ktl, qsl=qsl, hc=hc, p0=p0):
                        return t.matmul(
                            pS2[:, j * 512:(j + 1) * 512],
                            k_fm[win][p0:p0 + 64, hc * 512 + ktl * 128: hc * 512 + ktl * 128 + 128],
                            qsl, start=True, stop=True)
                    S.add("te", f, r=(f"k_fm{win}", "q_fm"), w=("lpS2",),
                          after=(f"rope_{w}_q_{hc}", f"rope_{w}_k_{hc}") if win == cw
                          else (f"rope_{w}_q_{hc}",),
                          label=f"lsim_{w}_{h}_{kt}")

                def f(s, par=par, n=len(pair)):
                    return s.activation(PT[par][:, 0:n * 512], pS2[:, 0:n * 512], AF.Exp,
                                        scale=0.125)
                S.add("se", f, r=("lpS2",), w=(f"lPT{par}",))

                def f(v, par=par, pair=tuple(pair), n=len(pair)):
                    kt0 = pair[0]
                    return v.tensor_tensor(PT[par][:, 0:n * 512], PT[par][:, 0:n * 512],
                                           C["lmask"][:, kt0 * 512:(kt0 + n) * 512],
                                           ALU.mult)
                S.add("ve", f, r=(f"lPT{par}", "c_lmask"), w=(f"lPT{par}",),
                      label=f"lptm_{w}_{h}_{bi}")

                for j, kt in enumerate(pair):
                    win = cw if kt >= 4 else 1 - cw
                    ktl = kt % 4
                    last = (bi + 2 >= len(kts)) and (j == len(pair) - 1)

                    def f(t, kt=kt, j=j, win=win, ktl=ktl, h=h, p0=p0, tp=tp,
                          par=par, first=first, last=last, pq=pq):
                        ins = t.matmul(pO[pq][p0:p0 + 64, :],
                                       v_tm[win][:, ktl * D + h * 64: ktl * D + h * 64 + 64],
                                       PT[par][:, j * 512:(j + 1) * 512],
                                       start=first and j == 0, stop=last,
                                       tile_position=tp)
                        return ins
                    S.add("te", f, r=(f"v_tm{win}", f"lPT{par}"), w=(f"lpO{pq}",),
                          label=f"lpv_{w}_{h}_{kt}")

                    def f(t, j=j, par=par, hp=hp, first=first, last=last):
                        return t.matmul(pA[hp * 32: hp * 32 + 1, 0:512], onescol_b[:],
                                        PT[par][:, j * 512:(j + 1) * 512],
                                        start=first and j == 0, stop=last,
                                        tile_position=(0, 32) if hp else None)
                    S.add("te", f, r=(f"lPT{par}", "c_onescol_bf"), w=("lpA",),
                          label=f"lrs_{w}_{h}_{kt}")
                first = False
            if hp == 1:
                def f(s):
                    s.activation(rsum[:, 0:512], pA[0:1, 0:512], AF.Ln)
                    return s.activation(rsum[:, 512:1024], pA[32:33, 0:512], AF.Ln)
                S.add("se", f, r=("lpA",), w=("rsum",))

                def f(s):
                    return s.activation(rsum[:], rsum[:], AF.Exp, scale=-1.0)
                S.add("se", f, r=("rsum",), w=("rsum",))

                def f(t):
                    t.matmul(pB[0:64, 0:512], ones_f[:, 0:64], rsum[:, 0:512],
                             start=True, stop=True)
                    return t.matmul(pB[64:128, 0:512], ones_f[:, 0:64], rsum[:, 512:1024],
                                    start=True, stop=True, tile_position=(0, 64))
                S.add("te", f, r=("rsum", "c_ones_f32"), w=("lpB",))

                def f(v):
                    return v.tensor_copy(rexp[:], pB[:, 0:512])
                S.add("ve", f, r=("lpB",), w=("rexp",))

                def f(v, hc=hc, pq=pq):
                    return v.tensor_tensor(o_fm[:, hc * 512:(hc + 1) * 512],
                                           pO[pq][:, :], rexp[:], ALU.mult)
                S.add("ve", f, r=(f"lpO{pq}", "rexp"), w=("o_fm",),
                      label=f"lofm_{w}_{hc}")
        if w == 0:
            add_dump(nc, S, "l_ofm", o_fm[:], tuple(f"lofm_{0}_{c}" for c in range(8)))
        for sl in range(2):
            ws_lo.load_slice(S, "se", wlo_s[:], sl, "wlo_s", label=f"lwlo_{w}_{sl}")
            for tt in range(4):
                pb, pbk = (pA, "lpA") if tt % 2 == 0 else (pB, "lpB")

                def f(t, tt=tt, pb=pb):
                    ins = None
                    for ch in range(8):
                        ins = t.matmul(pb[:, 0:512],
                                       o_fm[:, ch * 512 + tt * 128: ch * 512 + tt * 128 + 128],
                                       wlo_s[:, ch * 512:(ch + 1) * 512],
                                       start=(ch == 0), stop=(ch == 7))
                    return ins
                S.add("te", f, r=("o_fm", "wlo_s"),
                      w=(pbk,), after=tuple(f"lofm_{w}_{q}" for q in range(8)))

                def f(v, tt=tt, sl=sl, pb=pb):
                    return v.tensor_copy(lst[tt][:, sl * 512:(sl + 1) * 512],
                                         pb[:, 0:512])
                S.add("ve", f, r=(pbk,), w=(f"lst{tt}_{sl}",))
            if sl == 1:
                for tt in range(4):
                    tok = 4 * w + tt

                    def f(g, tt=tt, tok=tok):
                        dst = zacc_dram.rearrange("(n p) d -> n p d", p=128)[tok]
                        return g.dma_start(out=dst, in_=lst[tt][:])
                    f.dma_n = 1
                    S.add("gp", f, r=(f"lst{tt}_0", f"lst{tt}_1"),
                          w=(f"zacc_{tok}",), dma=True)


def emit_l0_ffn_combine(nc, S, stack, X, C, P, zacc_dram, stage, ws_extra=None):
    RK = ws_extra["recip_keep"] if ws_extra is not None else None
    sb = lambda n, s, d: stack.enter_context(nc.sbuf_tensor(n, s, d))
    zt = [sb(f"cz{j}", [128, D], BF16) for j in range(2)]
    ztf = sb("cztf", [128, D], F32)
    csq = sb("csq", [128, D], BF16)
    czs = sb("czs", [128, 8], F32)

    for tok in range(32):
        par = tok % 2

        def f(g, tok=tok, par=par):
            src = zacc_dram.rearrange("(n p) d -> n p d", p=128)[tok]
            return g.dma_start(out=zt[par][:], in_=src)
        f.dma_n = 1
        S.add("gp", f, r=(f"zacc_{tok}",), w=(f"cz{par}",), dma=True)

        if stage >= 4 and tok >= 4:
            col = (tok % 4) * 7 + (tok // 4 - 1)

            def f(v, tok=tok, par=par, col=col):
                v.tensor_scalar(zt[par][:], zt[par][:], RK[:, col:col + 1], None,
                                ALU.mult)
                return v.tensor_tensor(ztf[:], X[:, tok * D:(tok + 1) * D],
                                       zt[par][:], ALU.add)
            S.add("ve", f, r=(f"cz{par}", "X"), sr=("k_recip",),
                  w=("ztf", f"cz{par}"))
        else:
            def f(v, tok=tok, par=par):
                return v.tensor_tensor(ztf[:], X[:, tok * D:(tok + 1) * D],
                                       zt[par][:], ALU.add)
            S.add("ve", f, r=(f"cz{par}", "X"), w=("ztf",))
        if tok == 0:
            add_dump(nc, S, "c_zt0", zt[par][:], (f"cz{par}",))
            add_dump(nc, S, "c_ztf0", ztf[:], ("ztf",))

        def f(v):
            return v.tensor_reduce(czs[:, 2:3], ztf[:], AX.X, ALU.add)
        S.add("ve", f, r=("ztf",), w=("czs_s1",))

        def f(s):
            return s.activation(csq[:], ztf[:], AF.Square)
        S.add("se", f, r=("ztf",), w=("csq",))

        def f(v):
            return v.tensor_reduce(czs[:, 3:4], csq[:], AX.X, ALU.add)
        S.add("ve", f, r=("csq",), w=("czs_s2",))

        def f(v):
            return v.tensor_scalar_mul(czs[:, 0:1], czs[:, 2:3], 1.0 / D)
        S.add("ve", f, r=("czs_s1",), w=("czs_mu",))

        def f(s):
            return s.activation(czs[:, 4:5], czs[:, 0:1], AF.Square)
        S.add("se", f, r=("czs_mu",), w=("czs_musq",))

        def f(v):
            return v.scalar_tensor_tensor(czs[:, 5:6], czs[:, 3:4], 1.0 / D,
                                          czs[:, 4:5], ALU.mult, ALU.subtract)
        S.add("ve", f, r=("czs_s2", "czs_musq"), w=("czs_var",))

        def f(s):
            return s.activation(czs[:, 1:2], czs[:, 5:6], AF.Ln, bias=C["eps_ln"][:, :])
        S.add("se", f, r=("czs_var",), sr=("c_eps_ln",), w=("czs_rs0",))

        def f(s):
            return s.activation(czs[:, 1:2], czs[:, 1:2], AF.Exp, scale=-0.5)
        S.add("se", f, r=("czs_rs0",), w=("czs_rs",))

        def f(v, tok=tok):
            return v.tensor_scalar(X[:, tok * D:(tok + 1) * D], ztf[:],
                                   czs[:, 0:1], czs[:, 1:2], ALU.subtract, ALU.mult)
        S.add("ve", f, r=("ztf",), sr=("czs_mu", "czs_rs"), w=("X",))




def emit_l0_ffn_routed(nc, S, stack, X, C, P, zacc_dram, stage, ws_extra):
    S.paranoid = True
    sb = lambda n, s, d: stack.enter_context(nc.sbuf_tensor(n, s, d))
    psb = lambda n, s, d=F32: stack.enter_context(nc.psum_tensor(n, s, d))
    xr_pool = ws_extra["xr_pool"]
    x1f_dram = ws_extra["x1f_dram"]
    I32 = mybir.dt.int32
    I16 = mybir.dt.int16

    xsq = sb("r_xsq", [128, D], BF16)
    st3 = sb("r_st3", [128, 4], F32)
    nrmall = sb("r_nrm", [128, 32], F32)
    xrt = sb("r_xrt", [128, D], BF16)
    xff = sb("r_xff", [128, D], F32)
    x1t = [sb(f"r_x1t{j}", [128, D], F32) for j in range(2)]
    s_small = sb("r_ssm", [4, 128], F32)
    s_tm = sb("r_stm", [128, 128], F32)
    St = sb("r_St", [128, 448], F32)
    Ebuf = sb("r_E", [128, 448], F32)
    member = sb("r_mem", [128, 448], F32)
    prefx = sb("r_prefx", [128, 448], F32)
    counts = sb("r_counts", [128, 28], F32)
    recip = sb("r_recip", [128, 28], F32)
    av = sb("r_av", [1, 28], F32)
    tmp28 = sb("r_t28", [1, 28], F32)
    tmp28b = sb("r_t28b", [1, 28], F32)
    sum28 = sb("r_s28", [1, 28], F32)
    apat = sb("r_apat", [1, 448], F32)
    offs = sb("r_offs", [1, 448], F32)
    lo_i = sb("r_lo", [1, 28], I32)
    hi_i = sb("r_hi", [1, 28], I32)
    mid_i = sb("r_mid", [1, 28], I32)
    msk28 = sb("r_msk", [1, 28], F32)
    mski = sb("r_mski", [1, 28], I32)
    vstar = sb("r_vst", [1, 28], F32)
    needv = sb("r_need", [1, 28], F32)
    kk = sb("r_kk", [128, 448], F32)
    eE = sb("r_eE", [1, 28], F32)
    eM = sb("r_eM", [1, 28], F32)
    eEt = sb("r_eEt", [1, 28], F32)
    trc = sb("r_trc", [1, 31 * 112], F32)
    eMt = sb("r_eMt", [1, 28], F32)
    Mfac = sb("r_Mfac", [1, 28], F32)


    pQ3 = psb("r_pQ3", [128, 512])
    pT3 = psb("r_pT3", [128, 512])
    pTb3 = psb("r_pTb3", [128, 512], BF16)
    pP3 = psb("r_pP3", [128, 512])

    ones_f = C["ones_f32"]
    idf = C["identity_f32"]
    idb = C["identity_bf"]

    for n in range(32):
        Xt = X[:, n * D:(n + 1) * D]

        def f(s, Xt=Xt):
            return s.activation(xsq[:], Xt, AF.Square)
        S.add("se", f, r=("X",), w=("r_xsq",))

        def f(v):
            return v.tensor_reduce(st3[:, 0:1], xsq[:], AX.X, ALU.add)
        S.add("ve", f, r=("r_xsq",), w=("st3_ssq",))

        def f(s):
            return s.activation(st3[:, 3:4], st3[:, 0:1], AF.Ln)
        S.add("se", f, r=("st3_ssq",), w=("st3_ln",))

        def f(s):
            return s.activation(st3[:, 1:2], st3[:, 3:4], AF.Exp, scale=-0.5)
        S.add("se", f, r=("st3_ln",), w=("st3_rsn",))

        def f(s):
            return s.activation(st3[:, 2:3], st3[:, 3:4], AF.Exp, scale=0.5)
        S.add("se", f, r=("st3_ln",), w=("st3_nrm",))

        def f(v, n=n):
            return v.tensor_copy(nrmall[:, n:n + 1], st3[:, 2:3])
        S.add("ve", f, r=("st3_nrm",), w=("r_nrm",), label=f"nrm_{n}")

        def f(v, Xt=Xt):
            return v.tensor_scalar(xrt[:], Xt, st3[:, 1:2], None, ALU.mult)
        S.add("ve", f, r=("X",), sr=("st3_rsn",), w=("r_xrt",), label=f"xrt_{n}")

        def f(g, n=n):
            dst = xr_pool.rearrange("(n p) d -> n p d", p=128)[n]
            return g.dma_start(out=dst, in_=xrt[:])
        f.dma_n = 1
        S.add("gp", f, r=("r_xrt",), w=(f"xrp_{n}",), dma=True)

        parx = n % 2

        def f(sy, n=n, parx=parx):
            srcd = x1f_dram.rearrange("(n p) d -> n p d", p=128)[n]
            return sy.dma_start(out=x1t[parx][:], in_=srcd)
        f.dma_n = 1
        S.add("sy", f, r=(f"x1f_{n}_0", f"x1f_{n}_1"), w=(f"r_x1t{parx}",),
              dma=True)
        for half in range(2):
            def f(t, half=half, parx=parx):
                ins = None
                for q in range(4):
                    dch = half * 4 + q
                    ins = t.transpose(pT3[:, q * 128:(q + 1) * 128],
                                      x1t[parx][:, dch * 128:(dch + 1) * 128],
                                      idf[:])
                return ins
            S.add("te", f, r=(f"r_x1t{parx}", "c_identity_f32"), w=("r_pT3",))

            def f(v, half=half):
                return v.tensor_copy(xff[:, half * 512:(half + 1) * 512],
                                     pT3[:, 0:512])
            S.add("ve", f, r=("r_pT3",), w=("r_xff",))

        def f(t):
            ins = None
            for dch in range(8):
                ins = t.matmul(pQ3[0:4, 0:128], P["f_rtT"][:, dch * 4:(dch + 1) * 4],
                               xff[:, dch * 128:(dch + 1) * 128],
                               start=(dch == 0), stop=(dch == 7))
            return ins
        S.add("te", f, r=("r_xff", "p_f_rtT"), w=("r_pQ3",))

        def f(v):
            return v.tensor_copy(s_small[:], pQ3[0:4, 0:128])
        S.add("ve", f, r=("r_pQ3",), w=("r_ssm",))

        def f(t):
            return t.transpose(pT3[:, 0:4], s_small[:], idf[0:4, 0:4])
        S.add("te", f, r=("r_ssm", "c_identity_f32"), w=("r_pT3",))

        def f(v, n=n):
            return v.tensor_copy(s_tm[:, n * 4:(n + 1) * 4], pT3[:, 0:4])
        S.add("ve", f, r=("r_pT3",), w=("r_stm",))
    S.add("ve", lambda v: v.engine_nop(), r=("r_stm",), label="stm_done")

    def f(v):
        return v.memset(St[:], -1e30)
    S.add("ve", f, w=("r_St",))

    def f(v):
        ins = None
        for j in range(7):
            for g in range(2):
                nv = 4 * (j + 1)
                src_ap = s_tm[:].rearrange("p (n j) -> p n j", j=4)[:, 0:nv, g]
                dst_ap = St[:].rearrange("p (c r) -> p c r", r=14)[:, 0:nv, j * 2 + g]
                ins = v.tensor_copy(dst_ap, src_ap)
                srcq = s_tm[:].rearrange("p (n j) -> p n j", j=4)[:, (j + 1) * 4:(j + 2) * 4, 2 + g]
                dstq = St[:, 392:448].rearrange("p (c r) -> p c r", r=14)[:, :, j * 2 + g]
                ins = v.tensor_copy(dstq, srcq)
        return ins
    S.add("ve", f, r=("r_stm",), w=("r_St",), after=("stm_done",), label="st_built")

    def f(v):
        return v.memset(av[:], 0.0)
    S.add("ve", f, w=("r_av",))
    eps_sched = [max(4.0 * 0.7 ** i, 0.03) for i in range(20)]
    for it, ce in enumerate(eps_sched):
        def f(v):
            v.tensor_copy(apat[:, 0:392].rearrange("o (c r) -> o c r", r=14),
                          av[:, 0:14].unsqueeze(1).broadcast_to([1, 28, 14]))
            return v.tensor_copy(apat[:, 392:448].rearrange("o (c r) -> o c r", r=14),
                                 av[:, 14:28].unsqueeze(1).broadcast_to([1, 4, 14]))
        S.add("ve", f, r=("r_av",), w=("r_apat",))

        def f(t):
            return t.matmul(pT3[:, 0:448], ones_f[:], apat[:], start=True, stop=True)
        S.add("te", f, r=("r_apat", "c_ones_f32"), w=("r_pT3",))

        def f(v):
            return v.tensor_tensor(Ebuf[:], St[:], pT3[:, 0:448], ALU.add)
        S.add("ve", f, sr=("r_St", "r_pT3"), w=("r_E",))

        def f(v):
            return v.tensor_single_scalar(Ebuf[:], Ebuf[:], 0.0, ALU.min)
        S.add("ve", f, sr=("r_E",), w=("r_E",))

        def f(s, ce=ce):
            return s.activation(Ebuf[:], Ebuf[:], AF.Exp, scale=1.0 / ce)
        S.add("se", f, r=("r_E",), w=("r_E",))

        def f(t):
            return t.matmul(pQ3[0:1, 0:448], C["onescol_f32"][:], Ebuf[:],
                            start=True, stop=True)
        S.add("te", f, r=("r_E", "c_onescol_f32"), w=("r_pQ3",))

        def f(v):
            v.tensor_reduce(sum28[:, 0:14], pQ3[0:1, 0:392].rearrange("o (c r) -> o r c", r=14),
                            AX.X, ALU.add)
            return v.tensor_reduce(sum28[:, 14:28],
                                   pQ3[0:1, 392:448].rearrange("o (c r) -> o r c", r=14),
                                   AX.X, ALU.add)
        S.add("ve", f, r=("r_pQ3",), w=("r_s28",))

        def f(s):
            return s.activation(tmp28[:], sum28[:], AF.Ln)
        S.add("se", f, r=("r_s28",), w=("r_t28",))

        def f(v):
            return v.tensor_tensor(tmp28b[:], C["sel_logk"][:], tmp28[:],
                                   ALU.subtract)
        S.add("ve", f, sr=("r_t28", "c_sel_logk"), w=("r_t28b",))

        def f(v, ce=ce):
            return v.scalar_tensor_tensor(av[:], tmp28b[:], ce, av[:], ALU.mult,
                                          ALU.add)
        S.add("ve", f, sr=("r_t28b", "r_av"), w=("r_av",), label=f"cd_{it}")

    def f(v):
        v.tensor_copy(apat[:, 0:392].rearrange("o (c r) -> o c r", r=14),
                      av[:, 0:14].unsqueeze(1).broadcast_to([1, 28, 14]))
        return v.tensor_copy(apat[:, 392:448].rearrange("o (c r) -> o c r", r=14),
                             av[:, 14:28].unsqueeze(1).broadcast_to([1, 4, 14]))
    S.add("ve", f, r=("r_av",), w=("r_apat",))

    def f(t):
        return t.matmul(pT3[:, 0:448], ones_f[:], apat[:], start=True, stop=True)
    S.add("te", f, r=("r_apat", "c_ones_f32"), w=("r_pT3",))

    def f(v):
        return v.tensor_tensor(kk[:], St[:], pT3[:, 0:448], ALU.add)
    S.add("ve", f, sr=("r_St", "r_pT3"), w=("r_kk",))

    def f(v):
        return v.tensor_single_scalar(kk[:], kk[:], 0.0, ALU.min)
    S.add("ve", f, sr=("r_kk",), w=("r_kk",))

    def f(v):
        return v.tensor_scalar_mul(kk[:], kk[:], -1.0)
    S.add("ve", f, sr=("r_kk",), w=("r_kk",))

    def f(v):
        return v.tensor_single_scalar(kk[:], kk[:], 0.0, ALU.add)
    S.add("ve", f, sr=("r_kk",), w=("r_kk",), label="kk_done")

    Ef = av
    Ef = None
    def f(v):
        v.memset(eE[:], 0.0)
        return v.memset(eM[:], 0.0)
    S.add("ve", f, w=("r_eE", "r_eM"))

    LN2 = float(np.log(2.0))

    def add_testf_ops(S, rE, rM):
        def f(s):
            return s.activation(tmp28[:], eEt[:], AF.Exp, scale=LN2, bias=C["nl2b"][:, :])
        S.add("se", f, r=(rE,), sr=("c_nl2b",), w=("r_t28",))

        def f(v):
            return v.tensor_tensor(mid_i[:], tmp28[:].bitcast(mybir.dt.int32),
                                   C["expmask"][:], ALU.bitwise_and)
        S.add("ve", f, sr=("r_t28", "c_expmask"), w=("r_mid",))

        def f(v):
            return v.tensor_scalar(Mfac[:], eMt[:], 2.0 ** -23, 1.0, ALU.mult, ALU.add)
        S.add("ve", f, sr=(rM,), w=("r_Mfac",))

        def f(v):
            return v.tensor_tensor(vstar[:], mid_i[:].bitcast(F32), Mfac[:], ALU.mult)
        S.add("ve", f, sr=("r_mid", "r_Mfac"), w=("r_vst",))

    for b in range(30, -1, -1):
        def f(v, b=b):
            if b >= 23:
                return v.tensor_single_scalar(eEt[:], eE[:], float(2 ** (b - 23) - 1),
                                              ALU.add)
            return v.tensor_copy(eEt[:], eE[:])
        S.add("ve", f, sr=("r_eE",), w=("r_eEt",))

        def f(v, b=b):
            if b >= 23:
                return v.memset(eMt[:], float(2 ** 23 - 1))
            return v.tensor_single_scalar(eMt[:], eM[:], float(2 ** b - 1), ALU.add)
        S.add("ve", f, sr=("r_eM",), w=("r_eMt",))
        add_testf_ops(S, "r_eEt", "r_eMt")

        def f(v):
            mf = vstar[:]
            return v.tensor_copy(apat[:, 0:392].rearrange("o (c r) -> o c r", r=14),
                                 mf[:, 0:14].unsqueeze(1).broadcast_to([1, 28, 14]))
        S.add("ve", f, sr=("r_vst",), w=("r_apA",))

        def f(v):
            mf = vstar[:]
            return v.tensor_copy(apat[:, 392:448].rearrange("o (c r) -> o c r", r=14),
                                 mf[:, 14:28].unsqueeze(1).broadcast_to([1, 4, 14]))
        S.add("ve", f, sr=("r_vst", "r_apA"), w=("r_apat",))

        def f(t):
            return t.matmul(pT3[:, 0:448], ones_f[:], apat[:], start=True, stop=True)
        S.add("te", f, r=("r_apat", "c_ones_f32"), w=("r_pT3",))

        def f(v):
            return v.tensor_tensor(Ebuf[:], kk[:], pT3[:, 0:448], ALU.is_le)
        S.add("ve", f, sr=("r_kk", "r_pT3"), w=("r_E",))

        def f(t):
            return t.matmul(pQ3[0:1, 0:448], C["onescol_f32"][:], Ebuf[:],
                            start=True, stop=True)
        S.add("te", f, r=("r_E", "c_onescol_f32"), w=("r_pQ3",))

        def f(v):
            v.tensor_reduce(sum28[:, 0:14], pQ3[0:1, 0:392].rearrange("o (c r) -> o r c", r=14),
                            AX.X, ALU.add)
            return v.tensor_reduce(sum28[:, 14:28],
                                   pQ3[0:1, 392:448].rearrange("o (c r) -> o r c", r=14),
                                   AX.X, ALU.add)
        S.add("ve", f, r=("r_pQ3",), w=("r_s28",))

        def f(v):
            return v.tensor_tensor(msk28[:], sum28[:], C["sel_kth"][:], ALU.is_lt)
        S.add("ve", f, sr=("r_s28", "c_sel_kth"), w=("r_msk28",))

        def f(v, b=b):
            if b >= 23:
                return v.scalar_tensor_tensor(eE[:], msk28[:], float(2 ** (b - 23)),
                                              eE[:], ALU.mult, ALU.add)
            return v.scalar_tensor_tensor(eM[:], msk28[:], float(2 ** b),
                                          eM[:], ALU.mult, ALU.add)
        S.add("ve", f, sr=("r_msk28", "r_eE", "r_eM"),
              w=("r_eE",) if b >= 23 else ("r_eM",), label=f"bs_{b}")

    def f(v):
        v.tensor_copy(eEt[:], eE[:])
        return v.tensor_copy(eMt[:], eM[:])
    S.add("ve", f, r=("r_eE", "r_eM"), w=("r_eEt", "r_eMt"))
    add_testf_ops(S, "r_eEt", "r_eMt")
    S.add("ve", lambda v: v.engine_nop(), r=("r_vst",), label="vstar_b")
    def f(v):
        mf = vstar[:]
        v.tensor_copy(apat[:, 0:392].rearrange("o (c r) -> o c r", r=14),
                      mf[:, 0:14].unsqueeze(1).broadcast_to([1, 28, 14]))
        return v.tensor_copy(apat[:, 392:448].rearrange("o (c r) -> o c r", r=14),
                             mf[:, 14:28].unsqueeze(1).broadcast_to([1, 4, 14]))
    S.add("ve", f, r=("r_vst",), w=("r_apat",))

    def f(t):
        return t.matmul(pT3[:, 0:448], ones_f[:], apat[:], start=True, stop=True)
    S.add("te", f, r=("r_apat", "c_ones_f32"), w=("r_pT3",), label="vexp")

    def f(v):
        return v.tensor_tensor(member[:], kk[:], pT3[:, 0:448], ALU.is_lt)
    S.add("ve", f, r=("r_kk", "r_pT3"), w=("r_mem",))

    def f(t):
        return t.matmul(pQ3[0:1, 0:448], C["onescol_f32"][:], member[:],
                        start=True, stop=True)
    S.add("te", f, r=("r_mem", "c_onescol_f32"), w=("r_pQ3",))

    def f(v):
        v.tensor_reduce(sum28[:, 0:14], pQ3[0:1, 0:392].rearrange("o (c r) -> o r c", r=14),
                        AX.X, ALU.add)
        return v.tensor_reduce(sum28[:, 14:28],
                               pQ3[0:1, 392:448].rearrange("o (c r) -> o r c", r=14),
                               AX.X, ALU.add)
    S.add("ve", f, sr=("r_pQ3",), w=("r_s28",))

    def f(v):
        return v.tensor_tensor(needv[:], C["sel_kth"][:], sum28[:], ALU.subtract)
    S.add("ve", f, sr=("r_s28", "c_sel_kth"), w=("r_need",), label="needv")

    def f(v):
        return v.tensor_tensor(Ebuf[:], kk[:], pT3[:, 0:448], ALU.is_equal)
    S.add("ve", f, r=("r_kk", "r_pT3"), w=("r_E",), label="eq01")

    def f(t):
        return t.matmul(pP3[:, 0:448], C["triu_f32"][:], Ebuf[:], start=True, stop=True)
    S.add("te", f, r=("r_E", "c_triu_f32"), w=("r_pP3",))

    def f(t):
        return t.matmul(pQ3[0:1, 0:448], C["onescol_f32"][:], Ebuf[:], start=True, stop=True)
    S.add("te", f, r=("r_E", "c_onescol_f32"), w=("r_pQ3",))

    def f(v):
        return v.tensor_copy(apat[:], pQ3[0:1, 0:448])
    S.add("ve", f, sr=("r_pQ3",), w=("r_apat",))

    def f(v):
        ins = None
        for r in range(14):
            src0 = apat[:, 0:392].rearrange("o (c r) -> o r c", r=14)[:, r, :]
            dst0 = offs[:, 0:392].rearrange("o (c r) -> o r c", r=14)[:, r, :]
            ins = v.tensor_tensor_scan(dst0, src0, src0, 0.0, ALU.add, ALU.bypass)
        return ins
    S.add("ve", f, sr=("r_apat",), w=("r_offsA",))

    def f(v):
        ins = None
        for r in range(14):
            srcq = apat[:, 392:448].rearrange("o (c r) -> o r c", r=14)[:, r, :]
            dstq = offs[:, 392:448].rearrange("o (c r) -> o r c", r=14)[:, r, :]
            ins = v.tensor_tensor_scan(dstq, srcq, srcq, 0.0, ALU.add, ALU.bypass)
        return ins
    S.add("ve", f, sr=("r_apat", "r_offsA"), w=("r_offsB",))

    def f(v):
        return v.tensor_tensor(offs[:], offs[:], apat[:], ALU.subtract)
    S.add("ve", f, sr=("r_offsA", "r_offsB", "r_apat"), w=("r_offs",),
          label="offs")

    def f(t):
        return t.matmul(pT3[:, 0:448], ones_f[:], offs[:], start=True, stop=True)
    S.add("te", f, r=("r_offs", "c_ones_f32"), w=("r_pT3",), label="offexp")

    def f(v):
        return v.tensor_copy(prefx[:], pP3[:, 0:448])
    S.add("ve", f, sr=("r_pP3",), w=("r_prefx",))

    def f(v):
        return v.tensor_tensor(prefx[:], prefx[:], pT3[:, 0:448], ALU.add)
    S.add("ve", f, sr=("r_prefx", "r_pT3"), w=("r_prefx",))

    def f(v):
        mf = needv[:]
        v.tensor_copy(apat[:, 0:392].rearrange("o (c r) -> o c r", r=14),
                      mf[:, 0:14].unsqueeze(1).broadcast_to([1, 28, 14]))
        return v.tensor_copy(apat[:, 392:448].rearrange("o (c r) -> o c r", r=14),
                             mf[:, 14:28].unsqueeze(1).broadcast_to([1, 4, 14]))
    S.add("ve", f, r=("r_need",), w=("r_apat",))

    def f(t):
        return t.matmul(pP3[:, 0:448], ones_f[:], apat[:], start=True, stop=True)
    S.add("te", f, r=("r_apat", "c_ones_f32"), w=("r_pP3",), label="needexp")

    def f(v):
        return v.tensor_tensor(prefx[:], prefx[:], pP3[:, 0:448], ALU.is_le)
    S.add("ve", f, sr=("r_prefx", "r_pP3"), w=("r_prefx",))

    def f(v):
        return v.tensor_tensor(prefx[:], prefx[:], Ebuf[:], ALU.mult)
    S.add("ve", f, sr=("r_prefx", "r_E"), w=("r_prefx",))

    def f(v):
        return v.tensor_tensor(member[:], member[:], prefx[:], ALU.add)
    S.add("ve", f, sr=("r_mem", "r_prefx"), w=("r_mem",))

    def f(v):
        return v.tensor_single_scalar(Ebuf[:], St[:], -1e29, ALU.is_gt)
    S.add("ve", f, sr=("r_St", "r_mem"), w=("r_E",))

    def f(v):
        return v.tensor_tensor(member[:], member[:], Ebuf[:], ALU.mult)
    S.add("ve", f, sr=("r_mem", "r_E"), w=("r_mem",), label="member")

    def f(v):
        m4 = member[:, 392:448].rearrange("p (c j g) -> p c j g", j=7, g=2)
        return v.tensor_tensor(counts[:].rearrange("p (c j) -> p c j", j=7),
                               m4[:, :, :, 0], m4[:, :, :, 1], ALU.add)
    S.add("ve", f, r=("r_mem",), w=("r_counts",), label="counts")

    def f(s):
        return s.activation(recip[:], counts[:], AF.Ln, bias=1.0)
    S.add("se", f, r=("r_counts",), w=("r_recip",))

    def f(s):
        return s.activation(recip[:], recip[:], AF.Exp, scale=-1.0)
    S.add("se", f, r=("r_recip",), w=("r_recip",), label="recip")

    add_dump(nc, S, "r_stm_d", s_tm[:], ("r_stm",))
    add_dump(nc, S, "r_St_d", St[:], ("st_built",))
    add_dump(nc, S, "r_av_d", av[:], ("cd_19",))
    add_dump(nc, S, "r_kk_d", kk[:], ("kk_done",))
    add_dump(nc, S, "r_vst_d", vstar[:], ("vstar_b",))
    add_dump(nc, S, "r_mem_d", member[:], ("member",))
    add_dump(nc, S, "r_counts_d", counts[:], ("counts",))
    add_dump(nc, S, "r_recip_d", recip[:], ("recip",))
    add_dump(nc, S, "r_trc_d", trc[:], ("r_trc",))
    MK = ws_extra["member_keep"]
    RK = ws_extra["recip_keep"]

    def f(v):
        return v.tensor_copy(MK[:], member[:])
    S.add("ve", f, r=("r_mem",), w=("k_member",))

    def f(v):
        return v.tensor_copy(RK[:], recip[:])
    S.add("ve", f, r=("r_recip",), w=("k_recip",))
    S.paranoid = False




def emit_l0_ffn_rattn(nc, S, stack, X, C, P, zacc_dram, ws_extra):
    S.paranoid = True
    sb = lambda n, s, d: stack.enter_context(nc.sbuf_tensor(n, s, d))
    psb = lambda n, s, d=F32: stack.enter_context(nc.psum_tensor(n, s, d))
    xr_pool = ws_extra["xr_pool"]
    kp_dram = ws_extra["kp_dram"]
    vp_dram = ws_extra["vp_dram"]
    ws_wq = ws_extra["ws_wq"]
    ws_wkv = ws_extra["ws_wkv"]
    ws_wout = ws_extra["ws_wout"]
    member = ws_extra["member_keep"]

    ones_f = C["ones_f32"]
    onescol_b = C["onescol_bf"]
    idb = C["identity_bf"]

    pS = [psb(f"ra_pS{j}", [128, 512]) for j in range(2)]
    pO4 = [psb(f"ra_pO{j}", [128, 512]) for j in range(2)]
    pA = psb("ra_pA", [128, 512])
    pPr = psb("ra_pPr", [128, 512])
    pTb = psb("ra_pTb", [128, 512], BF16)
    pN4 = psb("ra_pN", [64, 512])

    pin = ws_extra["pin"]
    qgf = sb("ra_qgf", [128, 16], F32)
    cgf = sb("ra_cgf", [128, 16], F32)
    nkf = sb("ra_nkf", [128, 32], F32)
    nvf = sb("ra_nvf", [1, 2048], F32)
    qg32 = sb("ra_qg32", [128, 16], F32)
    cg32 = sb("ra_cg32", [128, 16], F32)
    nkb = sb("ra_nkb", [128, 32], BF16)
    nvb = sb("ra_nvb", [1, 2048], BF16)
    for nmf, buf in (("r_qg", qgf), ("r_cg", cgf), ("r_nk", nkf), ("r_nv", nvf)):
        def f(sy, nmf=nmf, buf=buf):
            return sy.dma_start(out=buf[:], in_=pin[nmf][:, :])
        f.dma_n = 1
        S.add("sy", f, w=(f"p_{nmf}",), dma=True)

    def f(v):
        return v.tensor_scalar_mul(qg32[:], qgf[:], 32.0)
    S.add("ve", f, sr=("p_r_qg",), w=("ra_qg32",))

    def f(v):
        return v.tensor_scalar_mul(cg32[:], cgf[:], 32.0)
    S.add("ve", f, sr=("p_r_cg",), w=("ra_cg32",))

    def f(v):
        return v.tensor_copy(nkb[:], nkf[:])
    S.add("ve", f, sr=("p_r_nk",), w=("ra_nkb",))

    def f(v):
        return v.tensor_copy(nvb[:], nvf[:])
    S.add("ve", f, sr=("p_r_nv",), w=("ra_nvb",))

    with contextlib.ExitStack() as r1s:
        sb1 = lambda n, s, d: r1s.enter_context(nc.sbuf_tensor(n, s, d))
        wkvb = [sb1(f"r1wkv{j}", [128, 8 * 512], BF16) for j in range(8)]
        xct = [sb1(f"r1xct{j}", [128, D], BF16) for j in range(2)]
        xcf_u = sb1("r1xcfu", [128, D], BF16)
        xcf_g = sb1("r1xcfg", [128, D], BF16)
        kst = [sb1(f"r1kst{j}", [128, D], BF16) for j in range(2)]
        vst = [sb1(f"r1vst{j}", [128, D], BF16) for j in range(2)]
        for sl in range(8):
            ws_wkv.load_slice(S, "sy", wkvb[sl][:], sl, f"r1wkv{sl}",
                              label=f"r1wkv_{sl}")
        for n in range(28):
            par = n % 2

            def f(sy, n=n, par=par):
                src = xr_pool.rearrange("(n p) d -> n p d", p=128)[n]
                return sy.dma_start(out=xct[par][:], in_=src)
            f.dma_n = 1
            S.add("sy", f, r=(f"xrp_{n}",), w=(f"r1xct{par}",), dma=True)
            for half in range(2):
                def f(t, half=half, par=par):
                    ins = None
                    for q in range(4):
                        dch = half * 4 + q
                        ins = t.transpose(pTb[:, q * 128:(q + 1) * 128],
                                          xct[par][:, dch * 128:(dch + 1) * 128],
                                          idb[:])
                    return ins
                S.add("te", f, r=(f"r1xct{par}", "c_identity_bf"), w=("ra_pTb",))

                def f(v, half=half):
                    return v.tensor_copy(xcf_u[:, half * 512:(half + 1) * 512],
                                         pTb[:, 0:512])
                S.add("ve", f, r=("ra_pTb",), w=("r1xcfu",))
            for g in range(2):
                par2 = (2 * n + g) % 2

                def f(v, g=g):
                    ins = None
                    for dch in range(8):
                        ins = v.tensor_scalar(xcf_g[:, dch * 128:(dch + 1) * 128],
                                              xcf_u[:, dch * 128:(dch + 1) * 128],
                                              cg32[:, g * 8 + dch:g * 8 + dch + 1],
                                              None, ALU.mult)
                    return ins
                S.add("ve", f, r=("r1xcfu",), sr=("ra_cg32",), w=("r1xcfg",))
                kkeys = []
                for och in range(8):
                    pb, pbk = (pS[0], "ra_pS0") if och % 2 == 0 else (pS[1], "ra_pS1")

                    def f(t, g=g, och=och, pb=pb):
                        ins = None
                        for dch in range(8):
                            wb = wkvb[4 * g + och // 4]
                            ins = t.matmul(
                                pb[:, 0:128],
                                wb[:, dch * 512 + (och % 4) * 128:
                                   dch * 512 + (och % 4) * 128 + 128],
                                xcf_g[:, dch * 128:(dch + 1) * 128],
                                start=(dch == 0), stop=(dch == 7))
                        return ins
                    S.add("te", f, r=("r1xcfg", f"r1wkv{4 * g + och // 4}"), w=(pbk,))

                    def f(v, och=och, par2=par2, pb=pb):
                        return v.tensor_copy(kst[par2][:, och * 128:(och + 1) * 128],
                                             pb[:, 0:128])
                    S.add("ve", f, r=(pbk,), w=(f"r1kst{par2}_{och}",))
                    kkeys.append(f"r1kst{par2}_{och}")
                vkeys = []
                for si in range(2):
                    pb, pbk = (pS[0], "ra_pS0") if si == 0 else (pS[1], "ra_pS1")

                    def f(t, g=g, si=si, pb=pb):
                        ins = None
                        for dch in range(8):
                            ins = t.matmul(pb[:, 0:512],
                                           xcf_g[:, dch * 128:(dch + 1) * 128],
                                           wkvb[4 * g + 2 + si][:, dch * 512:(dch + 1) * 512],
                                           start=(dch == 0), stop=(dch == 7))
                        return ins
                    S.add("te", f, r=("r1xcfg", f"r1wkv{4 * g + 2 + si}"), w=(pbk,))

                    def f(v, si=si, par2=par2, pb=pb):
                        return v.tensor_copy(vst[par2][:, si * 512:(si + 1) * 512],
                                             pb[:, 0:512])
                    S.add("ve", f, r=(pbk,), w=(f"r1vst{par2}_{si}",))
                    vkeys.append(f"r1vst{par2}_{si}")

                def f(gp, n=n, g=g, par2=par2):
                    return gp.dma_start(out=kp_dram[g, n], in_=kst[par2][:])
                f.dma_n = 1
                S.add("gp", f, r=tuple(kkeys), w=(f"kp_{g}_{n}",), dma=True)

                def f(gp, n=n, g=g, par2=par2):
                    return gp.dma_start(out=vp_dram[g, n], in_=vst[par2][:])
                f.dma_n = 1
                S.add("gp", f, r=tuple(vkeys), w=(f"vp_{g}_{n}",), dma=True)
    S.fence()

    with contextlib.ExitStack() as r2s:
        sb2 = lambda n, s, d: r2s.enter_context(nc.sbuf_tensor(n, s, d))
        wqb = [sb2(f"r2wq{j}", [128, 8 * 512], BF16) for j in range(2)]
        wob = [sb2(f"r2wo{j}", [128, 8 * 512], BF16) for j in range(2)]
        xqt = sb2("r2xqt", [128, 2 * D], BF16)
        xq_fm = sb2("r2xqfm", [128, 8 * 512], BF16)
        q_fm = sb2("r2qfm", [128, 8 * 512], BF16)
        Kh = sb2("r2Kh", [128, 28 * 512], BF16)
        Vh = sb2("r2Vh", [128, 28 * 512], BF16)
        PT4 = [sb2(f"r2PT{j}", [128, 512], BF16) for j in range(2)]
        nse = sb2("r2nse", [1, 1024], BF16)
        rs4 = sb2("r2rs", [1, 1024], F32)
        rexp4 = sb2("r2rexp", [128, 512], BF16)
        o_fm4 = xq_fm
        mb4 = sb2("r2mb", [128, 28], F32)
        memq4 = sb2("r2memq", [128, 4], F32)
        zrt = sb2("r2zrt", [128, D], BF16)
        zlocal = [sb2(f"r2zl{j}", [128, D], BF16) for j in range(2)]

        for g in range(2):
            for j in range(2):
                ws_wq.load_slice(S, "sy", wqb[j][:], 2 * g + j, f"r2wq{j}",
                                 label=f"r2wq_{g}_{j}")
                ws_wout.load_slice(S, "sy", wob[j][:], 2 * g + j, f"r2wo{j}",
                                   label=f"r2wo_{g}_{j}")
            for r in range(7):
                nkts = 4 * (r + 1)
                wt0 = 4 * (r + 1)
                for pairi in range(2):
                    def f(sy, wt0=wt0, pairi=pairi):
                        src = xr_pool.rearrange("(n p) d -> p n d", p=128)[
                            :, wt0 + 2 * pairi: wt0 + 2 * pairi + 2, :]
                        return sy.dma_start(
                            out=xqt[:].rearrange("p (n d) -> p n d", d=D), in_=src)
                    f.dma_n = 1
                    S.add("sy", f,
                          r=(f"xrp_{wt0 + 2 * pairi}", f"xrp_{wt0 + 2 * pairi + 1}"),
                          w=("r2xqt",), dma=True)
                    for ti in range(2):
                        tt = 2 * pairi + ti
                        for half in range(2):
                            def f(t, ti=ti, half=half):
                                ins = None
                                for q2 in range(4):
                                    dch = half * 4 + q2
                                    ins = t.transpose(
                                        pTb[:, q2 * 128:(q2 + 1) * 128],
                                        xqt[:, ti * D + dch * 128:
                                            ti * D + dch * 128 + 128], idb[:])
                                return ins
                            S.add("te", f, r=("r2xqt", "c_identity_bf"),
                                  w=("ra_pTb",))

                            def f(v, tt=tt, half=half, g=g):
                                ins = None
                                for q2 in range(4):
                                    dch = half * 4 + q2
                                    ins = v.tensor_scalar(
                                        xq_fm[:, dch * 512 + tt * 128:
                                              dch * 512 + tt * 128 + 128],
                                        pTb[:, q2 * 128:(q2 + 1) * 128],
                                        qg32[:, g * 8 + dch:g * 8 + dch + 1],
                                        None, ALU.mult)
                                return ins
                            S.add("ve", f, r=("ra_pTb",), sr=("ra_qg32",),
                                  w=("r2xqfm",))
                for och in range(8):
                    pb, pbk = (pS[0], "ra_pS0") if och % 2 == 0 else (pS[1], "ra_pS1")

                    def f(t, och=och, pb=pb):
                        ins = None
                        for ch in range(8):
                            wb = wqb[och // 4]
                            ins = t.matmul(
                                pb[:, 0:512],
                                wb[:, ch * 512 + (och % 4) * 128:
                                   ch * 512 + (och % 4) * 128 + 128],
                                xq_fm[:, ch * 512:(ch + 1) * 512],
                                start=(ch == 0), stop=(ch == 7))
                        return ins
                    S.add("te", f, r=("r2xqfm", f"r2wq{och // 4}"), w=(pbk,))

                    def f(v, och=och, pb=pb):
                        return v.tensor_copy(q_fm[:, och * 512:(och + 1) * 512],
                                             pb[:, 0:512])
                    S.add("ve", f, r=(pbk,), w=("r2qfm",))

                def f(v, g=g, r=r):
                    mv = member[:].rearrange("p (c j) -> p c j", j=14)[:, 0:28,
                                                                      2 * r + g]
                    return v.tensor_scalar(mb4[:], mv, 1.0, 30000.0,
                                           ALU.subtract, ALU.mult)
                S.add("ve", f, r=("k_member",), w=("r2mb",))

                def f(v, g=g, r=r):
                    mv = member[:, 392:448].rearrange("p (c j) -> p c j",
                                                      j=14)[:, :, 2 * r + g]
                    return v.tensor_copy(memq4[:], mv)
                S.add("ve", f, r=("k_member",), w=("r2memq",))
                for hs in range(2):
                    for kt in range(nkts):
                        def f(sy, g=g, kt=kt, hs=hs):
                            return sy.dma_start(
                                out=Kh[:, kt * 512:(kt + 1) * 512],
                                in_=kp_dram[g, kt][:, hs * 512:(hs + 1) * 512])
                        f.dma_n = 1
                        S.add("sy", f, r=(f"kp_{g}_{kt}",), w=(f"r2Kh_{kt}",),
                              dma=True)

                        def f(sy, g=g, kt=kt, hs=hs):
                            return sy.dma_start(
                                out=Vh[:, kt * 512:(kt + 1) * 512],
                                in_=vp_dram[g, kt][:, hs * 512:(hs + 1) * 512])
                        f.dma_n = 1
                        S.add("sy", f, r=(f"vp_{g}_{kt}",), w=(f"r2Vh_{kt}",),
                              dma=True)
                    for hcl in range(4):
                        hc = hs * 4 + hcl
                        pq = hcl % 2
                        for hp in range(2):
                            h = hc * 2 + hp
                            p0 = hp * 64
                            tp = (0, 64) if hp else None
                            for kt in range(nkts):
                                par = kt % 2

                                def f(t, kt=kt, hcl=hcl, p0=p0, par=par, hc=hc):
                                    return t.matmul(
                                        pS[par][:, 0:512],
                                        Kh[p0:p0 + 64, kt * 512 + hcl * 128:
                                           kt * 512 + hcl * 128 + 128],
                                        q_fm[p0:p0 + 64, hc * 512:hc * 512 + 512],
                                        start=True, stop=True)
                                S.add("te", f, r=(f"r2Kh_{kt}", "r2qfm"),
                                      w=(f"ra_pS{par}",))

                                def f(s, par=par, kt=kt):
                                    return s.activation(PT4[par][:, 0:512],
                                                        pS[par][:, 0:512], AF.Exp,
                                                        scale=0.125,
                                                        bias=mb4[:, kt:kt + 1])
                                S.add("se", f, r=(f"ra_pS{par}",), sr=("r2mb",),
                                      w=(f"r2PT{par}",))

                                def f(t, kt=kt, h=h, hs=hs, p0=p0, par=par, tp=tp,
                                      pq=pq):
                                    return t.matmul(
                                        pO4[pq][p0:p0 + 64, :],
                                        Vh[:, kt * 512 + (h - hs * 8) * 64:
                                           kt * 512 + (h - hs * 8) * 64 + 64],
                                        PT4[par][:, 0:512],
                                        start=(kt == 0), stop=False,
                                        tile_position=tp)
                                S.add("te", f, r=(f"r2Vh_{kt}", f"r2PT{par}"),
                                      w=(f"ra_pO{pq}",))

                                def f(t, par=par, hp=hp, kt=kt):
                                    return t.matmul(
                                        pA[hp * 32:hp * 32 + 1, 0:512],
                                        onescol_b[:], PT4[par][:, 0:512],
                                        start=(kt == 0), stop=False,
                                        tile_position=(0, 32) if hp else None)
                                S.add("te", f, r=(f"r2PT{par}", "c_onescol_bf"),
                                      w=("ra_pA",))

                            def f(t, g=g, h=h, p0=p0, hp=hp, hc=hc):
                                return t.matmul(
                                    pN4[hp * 32:hp * 32 + 1, 0:512],
                                    nkb[p0:p0 + 64, g * 16 + h:g * 16 + h + 1],
                                    q_fm[p0:p0 + 64, hc * 512:hc * 512 + 512],
                                    start=True, stop=True)
                            S.add("te", f, r=("ra_nkb", "r2qfm"), w=("ra_pN",))

                            def f(s, hp=hp):
                                return s.activation(
                                    nse[0:1, hp * 512:(hp + 1) * 512],
                                    pN4[hp * 32:hp * 32 + 1, 0:512], AF.Exp,
                                    scale=0.125)
                            S.add("se", f, r=("ra_pN",), w=("r2nse",))

                            def f(t, hp=hp):
                                return t.matmul(
                                    pA[hp * 32:hp * 32 + 1, 0:512],
                                    onescol_b[0:1, 0:1],
                                    nse[0:1, hp * 512:(hp + 1) * 512],
                                    start=False, stop=True,
                                    tile_position=(0, 32) if hp else None)
                            S.add("te", f, r=("r2nse", "c_onescol_bf"),
                                  w=("ra_pA",))

                            def f(t, g=g, h=h, p0=p0, tp=tp, pq=pq, hp=hp):
                                return t.matmul(
                                    pO4[pq][p0:p0 + 64, :],
                                    nvb[0:1, (g * 16 + h) * 64:
                                        (g * 16 + h) * 64 + 64],
                                    nse[0:1, hp * 512:(hp + 1) * 512],
                                    start=False, stop=True, tile_position=tp)
                            S.add("te", f, r=("r2nse", "ra_nvb"),
                                  w=(f"ra_pO{pq}",))
                            if hp == 1:
                                def f(s):
                                    return s.activation(rs4[:, 0:512],
                                                        pA[0:1, 0:512], AF.Ln)
                                S.add("se", f, sr=("ra_pA",), w=("r2rsa",))

                                def f(s):
                                    return s.activation(rs4[:, 512:1024],
                                                        pA[32:33, 0:512], AF.Ln)
                                S.add("se", f, sr=("ra_pA", "r2rsa"),
                                      w=("r2rs",))

                                def f(s):
                                    return s.activation(rs4[:], rs4[:], AF.Exp,
                                                        scale=-1.0)
                                S.add("se", f, r=("r2rs",), w=("r2rs",))

                                def f(t):
                                    t.matmul(pPr[0:64, 0:512], ones_f[:, 0:64],
                                             rs4[:, 0:512], start=True, stop=True)
                                    return t.matmul(pPr[64:128, 0:512],
                                                    ones_f[:, 0:64],
                                                    rs4[:, 512:1024], start=True,
                                                    stop=True,
                                                    tile_position=(0, 64))
                                S.add("te", f, r=("r2rs", "c_ones_f32"),
                                      w=("ra_pPr",))

                                def f(v):
                                    return v.tensor_copy(rexp4[:], pPr[:, 0:512])
                                S.add("ve", f, r=("ra_pPr",), w=("r2rexp",))

                                def f(v, hc=hc, pq=pq):
                                    return v.tensor_tensor(
                                        o_fm4[:, hc * 512:(hc + 1) * 512],
                                        pO4[pq][:, :], rexp4[:], ALU.mult)
                                S.add("ve", f, r=(f"ra_pO{pq}", "r2rexp"),
                                      w=("r2xqfm",), label=f"ro_{g}_{r}_{hc}")
                for tt in range(4):
                    for sl in range(2):
                        pb, pbk = (pS[0], "ra_pS0") if sl == 0 else (pS[1], "ra_pS1")

                        def f(t, tt=tt, sl=sl, pb=pb):
                            ins = None
                            for ch in range(8):
                                ins = t.matmul(
                                    pb[:, 0:512],
                                    o_fm4[:, ch * 512 + tt * 128:
                                          ch * 512 + tt * 128 + 128],
                                    wob[sl][:, ch * 512:(ch + 1) * 512],
                                    start=(ch == 0), stop=(ch == 7))
                            return ins
                        S.add("te", f, r=("r2xqfm", f"r2wo{sl}"), w=(pbk,),
                              after=tuple(f"ro_{g}_{r}_{q}" for q in range(8)))

                        def f(v, tt=tt, sl=sl, pb=pb):
                            return v.tensor_scalar(zrt[:, sl * 512:(sl + 1) * 512],
                                                   pb[:, 0:512],
                                                   memq4[:, tt:tt + 1], None,
                                                   ALU.mult)
                        S.add("ve", f, r=(pbk,), sr=("r2memq",), w=("r2zrt",))
                    tok = 4 * (r + 1) + tt
                    parz = tt % 2

                    def f(gp, tok=tok, parz=parz):
                        src = zacc_dram.rearrange("(n p) d -> n p d", p=128)[tok]
                        return gp.dma_start(out=zlocal[parz][:], in_=src)
                    f.dma_n = 1
                    S.add("gp", f, r=(f"zacc_{tok}",), w=(f"r2zl{parz}",),
                          dma=True)

                    def f(v, parz=parz):
                        return v.tensor_tensor(zlocal[parz][:], zrt[:],
                                               zlocal[parz][:], ALU.add)
                    S.add("ve", f, r=("r2zrt", f"r2zl{parz}"), w=(f"r2zl{parz}",))

                    def f(gp, tok=tok, parz=parz):
                        dst = zacc_dram.rearrange("(n p) d -> n p d", p=128)[tok]
                        return gp.dma_start(out=dst, in_=zlocal[parz][:])
                    f.dma_n = 1
                    S.add("gp", f, r=(f"r2zl{parz}",), w=(f"zacc_{tok}",),
                          dma=True)




def emit_l1_attn(nc, S, stack, X, C, P, ws_extra):
    S.paranoid = True
    sb = lambda n, s, d: stack.enter_context(nc.sbuf_tensor(n, s, d))
    psb = lambda n, s, d=F32: stack.enter_context(nc.psum_tensor(n, s, d))
    ag_in = ws_extra["ag_in"]
    ag_out = ws_extra["ag_out"]
    sems = ws_extra["sems"]
    pin = ws_extra["pin"]
    ws_l1qkv = ws_extra["ws_l1qkv"]
    x3f_dram = ws_extra["x1f_dram"]
    ones_f = C["ones_f32"]
    ones_b = C["ones_bf"]
    idb = C["identity_bf"]

    if "agsl" not in sems:
        sems["agsl"] = nc.alloc_semaphore("agsl")

    def f(g):
        return g.dma_start(out=ag_in.rearrange("(n p) d -> p n d", p=128),
                           in_=X[:].rearrange("p (n d) -> p n d", d=D))
    f.dma_n = 1
    S.add("gp", f, r=("X",), w=("ag_in",), dma=True)

    def f(g):
        return g.collective_compute(
            "AllGather", ALU.bypass, replica_groups=[list(range(NCORE))],
            ins=[ag_in[:, :].opt()], outs=[ag_out[:, :].opt()])
    S.add("gp", f, r=("ag_in",), w=("ag_done",), label="agc")

    def f(sy):
        for arm in nc.switch_core_id([sy], 128):
            ci = int(arm.logical) % NCORE
            srcd = ag_out.rearrange("(b c n p) d -> c p b n d", c=8, n=4,
                                    p=128)[ci]
            for b in range(8):
                sy.dma_start(
                    out=X[:, b * 4 * D:(b * 4 + 4) * D].rearrange(
                        "p (n d) -> p n d", d=D),
                    in_=srcd[:, b]).then_inc(sems["agsl"], 16)
        return sy.nop()
    S.add("sy", f, r=("ag_done", "X"), w=("ag_pend",))

    def f(sy):
        sy.wait_ge(sems["agsl"], 128)
        return sy.nop()
    S.add("sy", f, r=("ag_pend",), w=("X",), label="xexch")
    add_dump(nc, S, "l5_X0", X[:, 0:2 * D], ("X",))

    wq6 = [sb(f"l5w{j}", [128, 8 * 512], BF16) for j in range(6)]
    qkv_s = [sb(f"l5q{j}", [128, 3072], BF16) for j in range(8)]
    x1fm = sb("l5xfm", [128, D], BF16)
    bqf = sb("l5bqf", [1, 3072], F32)
    bqb = sb("l5bqb", [1, 3072], BF16)
    tmp5 = sb("l5tmp", [128, D], F32)
    runs = sb("l5runs", [128, 16], F32)
    sct = sb("l5sct", [128, 16], F32)
    runs_b = sb("l5runsb", [128, 16], BF16)
    ctx5 = sb("l5ctx", [128, D], F32)
    xf32 = sb("l5xf32", [128, D], F32)
    ztf5 = sb("l5ztf", [128, D], F32)
    zsq5 = sb("l5zsq", [128, D], BF16)
    zs5 = sb("l5zs", [128, 8], F32)

    pQ = psb("l5pQ", [128, 512])
    pR = psb("l5pR", [128, 512])
    pTb5 = psb("l5pTb", [128, 512], BF16)

    def f(sy):
        return sy.dma_start(out=bqf[:], in_=pin["l1_bqkv"][:, :])
    f.dma_n = 1
    S.add("sy", f, w=("p_l1bq",), dma=True)

    def f(v):
        return v.tensor_copy(bqb[:], bqf[:])
    S.add("ve", f, sr=("p_l1bq",), w=("l5bqb",))

    for sl in range(6):
        ws_l1qkv.load_slice(S, "sy", wq6[sl][:], sl, f"l5w{sl}",
                            label=f"l5w_{sl}")

    for pt in range(4):
        for b in range(8):
            m = b * 4 + pt
            for half in range(2):
                def f(t, m=m, half=half):
                    ins = None
                    for q2 in range(4):
                        dch = half * 4 + q2
                        ins = t.transpose(pTb5[:, q2 * 128:(q2 + 1) * 128],
                                          X[:, m * D + dch * 128:
                                            m * D + dch * 128 + 128], idb[:])
                    return ins
                S.add("te", f, r=("X", "c_identity_bf"), w=("l5pTb",),
                      after=("xexch",))

                def f(v, half=half):
                    return v.tensor_copy(x1fm[:, half * 512:(half + 1) * 512],
                                         pTb5[:, 0:512])
                S.add("ve", f, r=("l5pTb",), w=("l5xfm",))
            for sl in range(6):
                pb, pbk = (pQ, "l5pQ") if sl % 2 == 0 else (pR, "l5pR")

                def f(t, sl=sl, pb=pb):
                    ins = t.matmul(pb[:, 0:512], ones_b[:],
                                   bqb[0:1, sl * 512:(sl + 1) * 512],
                                   start=True, stop=False)
                    for ch in range(8):
                        ins = t.matmul(pb[:, 0:512],
                                       x1fm[:, ch * 128:(ch + 1) * 128],
                                       wq6[sl][:, ch * 512:(ch + 1) * 512],
                                       start=False, stop=(ch == 7))
                    return ins
                S.add("te", f, r=("l5xfm", f"l5w{sl}", "l5bqb", "c_ones_bf"),
                      w=(pbk,))

                def f(v, b=b, sl=sl, pb=pb):
                    return v.tensor_copy(qkv_s[b][:, sl * 512:(sl + 1) * 512],
                                         pb[:, 0:512])
                S.add("ve", f, r=(pbk,), w=(f"l5q{b}",))
            if pt == 0 and b == 0:
                add_dump(nc, S, "l5_q0", qkv_s[0][:], ("l5q0",))
        CS = 8 ** -0.5
        for s in range(8):
            for t in range(8):
                def f(v, s=s, t=t):
                    return v.tensor_tensor(tmp5[:], qkv_s[s][:, 0:1024],
                                           qkv_s[t][:, 1024:2048], ALU.mult)
                S.add("ve", f, r=(f"l5q{s}", f"l5q{t}"), w=("l5tmp",))

                if t == 0:
                    def f(v):
                        sc = tmp5[:].rearrange("p (h e) -> p h e", e=64)
                        return v.tensor_reduce(runs[:], sc, AX.X, ALU.add)
                    S.add("ve", f, sr=("l5tmp",), w=("l5runs",))
                else:
                    def f(v):
                        sc = tmp5[:].rearrange("p (h e) -> p h e", e=64)
                        return v.tensor_reduce(sct[:], sc, AX.X, ALU.add)
                    S.add("ve", f, sr=("l5tmp",), w=("l5sct",))

                    def f(v):
                        return v.tensor_add(runs[:], runs[:], sct[:])
                    S.add("ve", f, sr=("l5sct", "l5runs"), w=("l5runs",))

                def f(v):
                    return v.tensor_scalar_mul(runs_b[:], runs[:], 0.125)
                S.add("ve", f, sr=("l5runs",), w=("l5runsb",))

                def f(v, t=t):
                    vv = qkv_s[t][:, 2048:3072].rearrange("p (h e) -> p h e",
                                                          e=64)
                    rb = runs_b[:].unsqueeze(2).broadcast_to([128, 16, 64])
                    return v.tensor_tensor(
                        tmp5[:].rearrange("p (h e) -> p h e", e=64), rb, vv,
                        ALU.mult)
                S.add("ve", f, r=("l5runsb", f"l5q{t}"), w=("l5tmp",))

                def f(v, t=t, CS=CS):
                    if t == 0:
                        return v.tensor_scalar_mul(ctx5[:], tmp5[:], CS)
                    return v.scalar_tensor_tensor(ctx5[:], tmp5[:], CS, ctx5[:],
                                                  ALU.mult, ALU.add)
                S.add("ve", f, r=("l5tmp",) + (("l5ctx",) if t else ()),
                      w=("l5ctx",))
            if pt == 0 and s == 0:
                add_dump(nc, S, "l5_ctx0", ctx5[:], ("l5ctx",))
                add_dump(nc, S, "l5_runs0", runs[:], ("l5runs",))
            m = s * 4 + pt

            def f(v, m=m):
                return v.tensor_copy(xf32[:], X[:, m * D:(m + 1) * D])
            S.add("ve", f, r=("X",), w=("l5x32",))

            def f(v):
                return v.tensor_add(ztf5[:], ctx5[:], xf32[:])
            S.add("ve", f, r=("l5ctx", "l5x32"), w=("l5ztf",))

            def f(v):
                return v.tensor_reduce(zs5[:, 2:3], ztf5[:], AX.X, ALU.add)
            S.add("ve", f, r=("l5ztf",), w=("z5_s1",))

            def f(se):
                return se.activation(zsq5[:], ztf5[:], AF.Square)
            S.add("se", f, r=("l5ztf",), w=("z5_sq",))

            def f(v):
                return v.tensor_reduce(zs5[:, 3:4], zsq5[:], AX.X, ALU.add)
            S.add("ve", f, r=("z5_sq",), w=("z5_s2",))

            def f(v):
                return v.tensor_scalar_mul(zs5[:, 0:1], zs5[:, 2:3], 1.0 / D)
            S.add("ve", f, r=("z5_s1",), w=("z5_mu",))

            def f(se):
                return se.activation(zs5[:, 4:5], zs5[:, 0:1], AF.Square)
            S.add("se", f, r=("z5_mu",), w=("z5_musq",))

            def f(v):
                return v.scalar_tensor_tensor(zs5[:, 5:6], zs5[:, 3:4], 1.0 / D,
                                              zs5[:, 4:5], ALU.mult,
                                              ALU.subtract)
            S.add("ve", f, r=("z5_s2", "z5_musq"), w=("z5_var",))

            def f(se):
                return se.activation(zs5[:, 1:2], zs5[:, 5:6], AF.Ln,
                                     bias=C["eps_ln"][:, :])
            S.add("se", f, r=("z5_var",), sr=("c_eps_ln",), w=("z5_rs0",))

            def f(se):
                return se.activation(zs5[:, 1:2], zs5[:, 1:2], AF.Exp,
                                     scale=-0.5)
            S.add("se", f, r=("z5_rs0",), w=("z5_rs",))

            def f(v, m=m):
                return v.tensor_scalar(X[:, m * D:(m + 1) * D], ztf5[:],
                                       zs5[:, 0:1], zs5[:, 1:2], ALU.subtract,
                                       ALU.mult)
            S.add("ve", f, r=("l5ztf",), sr=("z5_mu", "z5_rs"), w=("X",))

            def f(v):
                return v.tensor_scalar(tmp5[:], ztf5[:], zs5[:, 0:1],
                                       zs5[:, 1:2], ALU.subtract, ALU.mult)
            S.add("ve", f, r=("l5ztf",), sr=("z5_mu", "z5_rs"), w=("l5tmp",))

            def f(g, m=m):
                dst = x3f_dram.rearrange("(n p) d -> n p d", p=128)[m]
                return g.dma_start(out=dst, in_=tmp5[:])
            f.dma_n = 1
            S.add("gp", f, r=("l5tmp",), w=(f"x3f_{m}",), dma=True)


def emit_l1_ffn(nc, S, stack, X, C, P, ws_extra):
    S.paranoid = True
    sb = lambda n, s, d: stack.enter_context(nc.sbuf_tensor(n, s, d))
    psb = lambda n, s, d=F32: stack.enter_context(nc.psum_tensor(n, s, d))
    pin = ws_extra["pin"]
    ws_m1 = ws_extra["ws_m1"]
    ws_m2 = ws_extra["ws_m2"]
    x3f_dram = ws_extra["x1f_dram"]
    ones_f = C["ones_f32"]
    ones_b = C["ones_bf"]
    idb = C["identity_bf"]
    idf = C["identity_f32"]

    wm1 = [sb(f"m6w1_{j}", [128, 8 * 512], BF16) for j in range(2)]
    wm2 = sb("m6w2", [128, 32 * 512], BF16)
    xfm_tb = sb("m6xfm", [128, 8 * 512], BF16)
    h_tb = sb("m6h", [128, 32 * 512], BF16)
    hbuf = sb("m6hb", [128, 512], BF16)
    xf3t = [sb("m6x3_0", [128, D], F32)] * 2
    xff3 = sb("m6xff", [128, D], F32)
    lg = sb("m6lg", [128, 16], F32)
    wk = sb("m6wk", [128, 16], F32)
    eqb = sb("m6eq", [128, 16], F32)
    msc = sb("m6msc", [128, 8], F32)
    maskt = sb("m6maskt", [128, 16], F32)
    mask_fm = sb("m6mfm", [16, 512], BF16)
    mrow4 = sb("m6mrow", [1, 1024], BF16)
    maskx = sb("m6maskx", [128, 16 * 512], BF16)
    rwT = sb("m6rwT", [128, 128], F32)
    rbf = sb("m6rb", [1, 16], F32)
    b1f = sb("m6b1", [128, 32], F32)
    b2f = sb("m6b2f", [1, 1024], F32)
    b2b = sb("m6b2b", [1, 1024], BF16)
    zmo = sb("m6zmo", [128, 4 * D], BF16)
    ztf6 = sb("m6ztf", [128, D], F32)
    zs6 = sb("m6zs", [128, 8], F32)

    pW = [psb(f"m6pW{j}", [128, 512]) for j in range(2)]
    pTf = psb("m6pTf", [128, 512])
    pTb6 = psb("m6pTb", [128, 512], BF16)
    pM = psb("m6pM", [128, 512])
    mask_dram = ws_extra["mask_dram"]

    for nm, buf in (("l1_rwT", rwT), ("l1_rb", rbf), ("l1_b1", b1f),
                    ("l1_b2", b2f)):
        def f(sy, nm=nm, buf=buf):
            return sy.dma_start(out=buf[:], in_=pin[nm][:, :])
        f.dma_n = 1
        S.add("sy", f, w=(f"p_{nm}",), dma=True)

    def f(v):
        return v.tensor_copy(b2b[:], b2f[:])
    S.add("ve", f, sr=("p_l1_b2",), w=("m6b2b",))

    for tb in range(8):
        for ti in range(4):
            m = tb * 4 + ti
            par = m % 2

            def f(sy, m=m):
                srcd = x3f_dram.rearrange("(n p) d -> n p d", p=128)[m]
                return sy.dma_start(out=xf3t[0][:], in_=srcd)
            f.dma_n = 1
            S.add("sy", f, r=(f"x3f_{m}",), w=("m6x3_0",), dma=True)
            for half in range(2):
                def f(t, half=half):
                    ins = None
                    for q2 in range(4):
                        dch = half * 4 + q2
                        ins = t.transpose(pTf[:, q2 * 128:(q2 + 1) * 128],
                                          xf3t[0][:, dch * 128:dch * 128 + 128],
                                          idf[:])
                    return ins
                S.add("te", f, r=("m6x3_0", "c_identity_f32"), w=("m6pTf",))

                def f(v, half=half):
                    return v.tensor_copy(xff3[:, half * 512:(half + 1) * 512],
                                         pTf[:, 0:512])
                S.add("ve", f, r=("m6pTf",), w=("m6xff",))

            def f(t):
                ins = t.matmul(pM[:, 0:16], ones_f[:], rbf[:], start=True,
                               stop=False)
                for ch in range(8):
                    ins = t.matmul(pM[:, 0:16],
                                   xff3[:, ch * 128:(ch + 1) * 128],
                                   rwT[:, ch * 16:(ch + 1) * 16],
                                   start=False, stop=(ch == 7))
                return ins
            S.add("te", f, r=("m6xff", "m6rwT", "m6rb", "c_ones_f32"),
                  w=("m6pM",))

            def f(v):
                return v.tensor_copy(lg[:], pM[:, 0:16])
            S.add("ve", f, r=("m6pM",), w=("m6lg",))

            def f(v):
                return v.tensor_copy(wk[:], lg[:])
            S.add("ve", f, r=("m6lg",), w=("m6wk",))
            for it in range(4):
                def f(v):
                    return v.tensor_reduce(msc[:, 0:1], wk[:], AX.X, ALU.max)
                S.add("ve", f, sr=("m6wk",), w=("m6msc",))
                if it < 3:
                    def f(v):
                        return v.tensor_scalar(eqb[:], wk[:], msc[:, 0:1],
                                               None, ALU.is_equal)
                    S.add("ve", f, sr=("m6wk", "m6msc"), w=("m6eq",))

                    def f(v):
                        return v.scalar_tensor_tensor(wk[:], eqb[:], -1e30,
                                                      wk[:], ALU.mult, ALU.add)
                    S.add("ve", f, sr=("m6eq", "m6wk"), w=("m6wk",))

            def f(v):
                return v.tensor_scalar(maskt[:], lg[:], msc[:, 0:1], None,
                                       ALU.is_ge)
            S.add("ve", f, sr=("m6lg", "m6msc"), w=("m6maskt",))

            def f(t):
                return t.transpose(pTf[0:16, 0:128], maskt[:], idf[:])
            S.add("te", f, r=("m6maskt", "c_identity_f32"), w=("m6pTf",))

            def f(v, ti=ti):
                return v.tensor_copy(mask_fm[:, ti * 128:(ti + 1) * 128],
                                     pTf[0:16, 0:128])
            S.add("ve", f, r=("m6pTf",), w=("m6mfm",))
            for half in range(2):
                def f(t, m=m, half=half):
                    ins = None
                    for q2 in range(4):
                        dch = half * 4 + q2
                        ins = t.transpose(pTb6[:, q2 * 128:(q2 + 1) * 128],
                                          X[:, m * D + dch * 128:
                                            m * D + dch * 128 + 128], idb[:])
                    return ins
                S.add("te", f, r=("X", "c_identity_bf"), w=("m6pTb",))

                def f(v, ti=ti, half=half):
                    ins = None
                    for q2 in range(4):
                        dch = half * 4 + q2
                        ins = v.tensor_copy(
                            xfm_tb[:, dch * 512 + ti * 128:
                                   dch * 512 + ti * 128 + 128],
                            pTb6[:, q2 * 128:(q2 + 1) * 128])
                    return ins
                S.add("ve", f, r=("m6pTb",), w=("m6xfm",))
        def f(g):
            return g.dma_start(out=mask_dram[:, :], in_=mask_fm[:])
        f.dma_n = 1
        S.add("gp", f, r=("m6mfm",), w=("m6mdr",), dma=True)
        for eg in range(8):
            def f(sy, eg=eg):
                return sy.dma_start(
                    out=mrow4[:].rearrange("o (e t) -> o e t", t=512),
                    in_=mask_dram[eg * 2:eg * 2 + 2, :])
            f.dma_n = 1
            S.add("sy", f, r=("m6mdr",), w=("m6mrow",), dma=True)
            for el in range(2):
                e = eg * 2 + el

                def f(t, el=el):
                    return t.matmul(pM[:, 0:512], ones_b[:],
                                    mrow4[0:1, el * 512:(el + 1) * 512],
                                    start=True, stop=True)
                S.add("te", f, r=("m6mrow", "c_ones_bf"), w=("m6pM",))

                def f(v, e=e):
                    return v.tensor_copy(maskx[:, e * 512:(e + 1) * 512],
                                         pM[:, 0:512])
                S.add("ve", f, r=("m6pM",), w=("m6maskx",))
        for sl in range(8):
            wpar = sl % 2
            ws_m1.load_slice(S, "sy", wm1[wpar][:], sl, f"m6w1k{wpar}",
                             label=f"m6w1_{tb}_{sl}")
            for ol in range(4):
                och = sl * 4 + ol
                e = och // 2
                pb, pbk = (pW[0], "m6pW0") if ol % 2 == 0 else (pW[1], "m6pW1")

                def f(t, och=och, ol=ol, wpar=wpar, pb=pb):
                    ins = None
                    for ch in range(8):
                        ins = t.matmul(
                            pb[:, 0:512],
                            wm1[wpar][:, ch * 512 + ol * 128:
                                      ch * 512 + ol * 128 + 128],
                            xfm_tb[:, ch * 512:(ch + 1) * 512],
                            start=(ch == 0), stop=(ch == 7))
                    return ins
                S.add("te", f, r=(f"m6w1k{wpar}", "m6xfm"), w=(pbk,))

                def f(se, och=och, pb=pb):
                    return se.activation(hbuf[:], pb[:, 0:512], AF.Relu,
                                         bias=b1f[:, och:och + 1])
                S.add("se", f, r=(pbk,), sr=("p_l1_b1",), w=("m6hbk",))

                def f(v, och=och, e=e):
                    return v.tensor_tensor(h_tb[:, och * 512:(och + 1) * 512],
                                           hbuf[:],
                                           maskx[:, e * 512:(e + 1) * 512],
                                           ALU.mult)
                S.add("ve", f, r=("m6hbk", "m6maskx"), w=("m6h",))
        for sl2 in range(2):
            ws_m2.load_slice(S, "sy", wm2[:], sl2, "m6w2k",
                             label=f"m6w2_{tb}_{sl2}")
            for tt in range(4):
                m = tb * 4 + tt
                pb, pbk = (pW[0], "m6pW0") if tt % 2 == 0 else (pW[1], "m6pW1")

                def f(t, tt=tt, sl2=sl2, pb=pb):
                    ins = t.matmul(pb[:, 0:512], ones_b[:],
                                   b2b[0:1, sl2 * 512:(sl2 + 1) * 512],
                                   start=True, stop=False)
                    for och in range(32):
                        ins = t.matmul(
                            pb[:, 0:512],
                            h_tb[:, och * 512 + tt * 128:
                                 och * 512 + tt * 128 + 128],
                            wm2[:, och * 512:(och + 1) * 512],
                            start=False, stop=(och == 31))
                    return ins
                S.add("te", f, r=("m6h", "m6w2k", "m6b2b", "c_ones_bf"),
                      w=(pbk,))

                def f(v, sl2=sl2, tt=tt, pb=pb):
                    return v.tensor_copy(
                        zmo[:, tt * D + sl2 * 512: tt * D + sl2 * 512 + 512],
                        pb[:, 0:512])
                S.add("ve", f, r=(pbk,), w=(f"m6zmo_{tt}_{sl2}",))
        for tt in range(4):
            m = tb * 4 + tt

            def f(v, tt=tt, m=m):
                return v.tensor_add(ztf6[:], zmo[:, tt * D:(tt + 1) * D],
                                    X[:, m * D:(m + 1) * D])
            S.add("ve", f, r=(f"m6zmo_{tt}_0", f"m6zmo_{tt}_1", "X"),
                  w=("m6ztf",))

            def f(v):
                return v.tensor_reduce(zs6[:, 2:3], ztf6[:], AX.X, ALU.add)
            S.add("ve", f, r=("m6ztf",), w=("z6_s1",))

            def f(se):
                return se.activation(xff3[:], ztf6[:], AF.Square)
            S.add("se", f, r=("m6ztf",), w=("m6xff",))

            def f(v):
                return v.tensor_reduce(zs6[:, 3:4], xff3[:], AX.X, ALU.add)
            S.add("ve", f, r=("m6xff",), w=("z6_s2",))

            def f(v):
                return v.tensor_scalar_mul(zs6[:, 0:1], zs6[:, 2:3], 1.0 / D)
            S.add("ve", f, r=("z6_s1",), w=("z6_mu",))

            def f(se):
                return se.activation(zs6[:, 4:5], zs6[:, 0:1], AF.Square)
            S.add("se", f, r=("z6_mu",), w=("z6_musq",))

            def f(v):
                return v.scalar_tensor_tensor(zs6[:, 5:6], zs6[:, 3:4], 1.0 / D,
                                              zs6[:, 4:5], ALU.mult,
                                              ALU.subtract)
            S.add("ve", f, r=("z6_s2", "z6_musq"), w=("z6_var",))

            def f(se):
                return se.activation(zs6[:, 1:2], zs6[:, 5:6], AF.Ln,
                                     bias=C["eps_ln"][:, :])
            S.add("se", f, r=("z6_var",), sr=("c_eps_ln",), w=("z6_rs0",))

            def f(se):
                return se.activation(zs6[:, 1:2], zs6[:, 1:2], AF.Exp,
                                     scale=-0.5)
            S.add("se", f, r=("z6_rs0",), w=("z6_rs",))

            def f(v, m=m):
                return v.tensor_scalar(X[:, m * D:(m + 1) * D], ztf6[:],
                                       zs6[:, 0:1], zs6[:, 1:2], ALU.subtract,
                                       ALU.mult)
            S.add("ve", f, r=("m6ztf",), sr=("z6_mu", "z6_rs"), w=("X",))


KSTAGE = 6


